# version 14
# baseline (speedup 1.0000x reference)
import numpy as np
import concourse.bass as bass
import concourse.mybir as mybir
from concourse.bass_utils import run_bass_kernel_spmd

F32 = mybir.dt.float32
BF16 = mybir.dt.bfloat16
AF = mybir.ActivationFunctionType
ALU = mybir.AluOpType
AX = mybir.AxisListType
NPBF16 = mybir.dt.np(BF16)

B = 16; S = 2048; DM = 1024
NH = 16; KVH = 4; HD = 64; QL = 64; KVL = 32
NE = 8; FF = 4096
NCORES = 8; SPC = 2
NQ = 4; QI = 1024
TT = 128; NTT = S // TT; GT = SPC * NTT
EPS_LN = 1e-5; EPS_GATE = 1e-6
T_TOTAL = float(B * S)
SCALE = float(HD) ** -0.5

ENGINES = ("sync", "tensor", "vector", "scalar", "gpsimd")


class Sem:
    __slots__ = ("name", "n", "h")

    def __init__(self, name):
        self.name, self.n, self.h = name, 0, None


class Prog:
    def __init__(self):
        self.streams = {e: [] for e in ENGINES}
        self.waited = {e: {} for e in ENGINES}
        self.sems = []
        self.mono = {e: self.sem("m_" + e) for e in ENGINES}

    def sem(self, name):
        s = Sem(name)
        self.sems.append(s)
        return s

    def emit(self, eng, fn, waits=(), inc=None, n=1):
        ops, w = self.streams[eng], self.waited[eng]
        for t in waits:
            if t is None:
                continue
            sm, v = t
            if v > w.get(sm, 0):
                w[sm] = v
                ops.append(("w", sm, v))
        if inc is not None:
            inc.n += n
            ops.append(("o", fn, inc, n))
            return (inc, inc.n)
        ops.append(("o", fn, None, 0))
        return None

    def pe(self, fn, waits=(), sync=True):
        return self.emit("tensor", fn, waits, self.mono["tensor"] if sync else None)

    def v(self, fn, waits=(), sync=True):
        return self.emit("vector", fn, waits, self.mono["vector"] if sync else None)

    def a(self, fn, waits=(), sync=True):
        return self.emit("scalar", fn, waits, self.mono["scalar"] if sync else None)

    def g(self, fn, waits=(), sync=True):
        return self.emit("gpsimd", fn, waits, self.mono["gpsimd"] if sync else None)

    def dma(self, eng, fn, waits=(), qsem=None):
        return self.emit(eng, fn, waits, qsem, 16)

    def play(self, block):
        def body(name):
            def run(e):
                for it in self.streams[name]:
                    if it[0] == "w":
                        e.wait_ge(it[1].h, it[2])
                    else:
                        ins = it[1](e)
                        if it[2] is not None:
                            ins.then_inc(it[2].h, it[3])
            return run
        block.sync(body("sync"))
        block.tensor(body("tensor"))
        block.vector(body("vector"))
        block.scalar(body("scalar"))
        block.gpsimd(body("gpsimd"))


def _rope_tables():
    inv_freq = 1.0 / (10000.0 ** (np.arange(0, HD, 2, dtype=np.float64) / HD))
    p = np.arange(128)
    i = (p % HD) // 2
    ang = np.outer(inv_freq[i], np.arange(S, dtype=np.float64))
    return (np.cos(ang).astype(np.float32).astype(NPBF16),
            np.sin(ang).astype(np.float32).astype(NPBF16))


def _mask4():
    m = np.zeros((128, 4 * 512), dtype=np.float32)
    pp = np.arange(128)[:, None]
    qq = np.arange(512)[None, :]
    for r in range(4):
        m[:, r * 512:(r + 1) * 512] = (128 * r + pp <= qq).astype(np.float32)
    return m.astype(NPBF16)


class Mem:
    BASE = 16384 + 256

    def __init__(self, nc):
        self.nc, self.off = nc, self.BASE

    def at(self, name, off, shape, dtype):
        return self.nc.alloc_sbuf_tensor_at(name, list(shape), dtype, offset=off)

    def alloc(self, name, shape, dtype):
        bpe = 2 if dtype == BF16 else 4
        nby = (int(np.prod(shape[1:])) * bpe + 31) & ~31
        t = self.at(name, self.off, shape, dtype)
        self.off += nby
        return t


def build_program(n_cores=NCORES, stages=("p1", "p2", "ffn", "attn", "combine"),
                  debug=()):
    st = set(stages)
    nc = bass.Bass("TRN2", target_bir_lowering=False)

    x_h = nc.dram_tensor("L0_attn_x", [SPC, S, DM], F32, kind="ExternalInput").ap()
    wqa_h = nc.dram_tensor("L0_attn_Wqa", [DM, QL], F32, kind="ExternalInput").ap()
    wqb_h = nc.dram_tensor("L0_attn_Wqb", [QL, DM], F32, kind="ExternalInput").ap()
    wkva_h = nc.dram_tensor("L0_attn_Wkva", [DM, KVL], F32, kind="ExternalInput").ap()
    wkvb_h = nc.dram_tensor("L0_attn_Wkvb", [KVL, KVH * HD * 2], F32, kind="ExternalInput").ap()
    wout_h = nc.dram_tensor("L0_attn_Wout", [DM, DM], F32, kind="ExternalInput").ap()
    bout_h = nc.dram_tensor("L0_attn_bout", [DM], F32, kind="ExternalInput").ap()
    wg_h = nc.dram_tensor("L0_ffn_Wg", [DM, NE], F32, kind="ExternalInput").ap()
    bg_h = nc.dram_tensor("L0_ffn_bg", [NE], F32, kind="ExternalInput").ap()
    w1_h = nc.dram_tensor("L0_ffn_W1", [NE, DM, FF], F32, kind="ExternalInput").ap()
    w2_h = nc.dram_tensor("L0_ffn_W2", [NE, FF, DM], F32, kind="ExternalInput").ap()
    out_h = nc.dram_tensor("out", [SPC, S, DM], F32, kind="ExternalOutput").ap()

    cc_np, ss_np = _rope_tables()
    cc_c = nc.inline_tensor(cc_np, name="cc_const")
    ss_c = nc.inline_tensor(ss_np, name="ss_const")
    mask_c = nc.inline_tensor(_mask4(), name="mask_const")
    ident_c = nc.inline_tensor(np.eye(128, dtype=np.float32), name="ident_const")
    onesb_c = nc.inline_tensor(np.ones((128, 128), dtype=np.float32).astype(NPBF16), name="onesb_const")
    onesf_c = nc.inline_tensor(np.ones((1, 128), dtype=np.float32), name="onesf_const")
    onesc_c = nc.inline_tensor(np.ones((128, 1), dtype=np.float32), name="onesc_const")

    spillA = nc.dram_tensor("spillA", [128, 8 * S], BF16).ap()
    cc_in_d = nc.dram_tensor("cc_in", [1, NE], F32)
    cc_out_d = nc.dram_tensor("cc_out", [1, NE], F32)

    dbg = {}
    for name, shape, dt_ in (("dbg_xn", [128, 8 * S], BF16),
                             ("dbg_gate", [128, NE * GT], F32),
                             ("dbg_yacc", [128, NTT * DM], BF16),
                             ("dbg_att", [128, 8 * S], BF16)):
        if name in debug:
            dbg[name] = nc.dram_tensor(name, shape, dt_, kind="ExternalOutput").ap()

    mem = Mem(nc)
    XN = mem.alloc("xn_fm", [128, 8 * S], BF16)
    ATT = mem.alloc("attno", [128, 8 * S], BF16)
    YACC = mem.alloc("yacc", [128, NTT * DM], BF16)
    BOUTB = mem.alloc("bout_bc", [128, DM], F32)
    IDENT = mem.alloc("ident", [128, 128], F32)
    ONESB = mem.alloc("ones_bf", [128, 128], BF16)
    ONESF = mem.alloc("ones_f32", [1, 128], F32)
    ONESC = mem.alloc("ones_col", [128, 1], F32)
    WQA = mem.alloc("wqa", [128, 8 * QL], BF16)
    WQB = mem.alloc("wqb", [QL, DM], BF16)
    WQB2 = mem.alloc("wqb2", [QL, DM], BF16)
    WKVB32 = mem.alloc("wkvb32", [KVL, KVH * HD * 2], F32)
    WKD = mem.alloc("wkd", [KVL, KVH * 128], BF16)
    WKD2 = mem.alloc("wkd2", [KVL, KVH * 128], BF16)
    WV = mem.alloc("wv", [KVL, KVH * HD], BF16)
    WKVA = mem.alloc("wkva", [128, 8 * KVL], BF16)
    WG = mem.alloc("wg", [128, 8 * NE], F32)
    BGR = mem.alloc("bg_row", [1, NE], F32)
    MASKED = mem.alloc("masked_tm", [128, NE * GT], F32)
    GATE = mem.alloc("gate_tm", [128, NE * GT], F32)
    DENOM = mem.alloc("denom", [1, NE], F32)
    BCG = mem.alloc("bc_gate", [128, NE], F32)
    BNST = [mem.alloc(f"bnst{i}", [128, 12], F32) for i in range(2)]
    MV = [mem.alloc(f"mv{i}", [128, 2], F32) for i in range(2)]
    LNV = [mem.alloc(f"lnv{i}", [128, 1], F32) for i in range(2)]
    RSTD = [mem.alloc(f"rstd{i}", [128, 1], F32) for i in range(2)]
    NMR = [mem.alloc(f"nmr{i}", [128, 1], F32) for i in range(2)]
    VEPS2 = [mem.alloc(f"veps{i}", [128, 1], F32) for i in range(2)]
    NMX = mem.alloc("nmx", [128, 1], F32)
    SEXP = mem.alloc("sexp", [128, 1], F32)
    RSUM = mem.alloc("rsum", [128, 1], F32)
    VEPS = mem.alloc("veps", [128, 1], F32)
    ZCOL = mem.alloc("zcol", [128, 1], F32)
    PEXP = mem.alloc("pexp", [128, NE], F32)
    PMX = mem.alloc("pmx", [128, 1], F32)
    MSK8 = mem.alloc("msk8", [128, NE], F32)
    PNM = mem.alloc("pnm", [128, NE], F32)
    MS1 = mem.alloc("msum1", [1, NE * GT], F32)
    RROW = mem.alloc("rrow", [1, NE], F32)

    AR = (mem.off + 31) & ~31
    assert AR + 94208 <= 229200, f"SBUF over budget: {AR + 91136}"

    W1S = mem.at("w1s", AR + 0, [128, 8 * QI], BF16)
    W2S = mem.at("w2s", AR + 16384, [128, 8 * QI], BF16)
    HSB = mem.at("hsb", AR + 32768, [128, 8 * 2048], BF16)
    QQB = mem.at("q_qb", AR + 0, [128, 8 * 512], BF16)
    K2X = mem.at("k2x", AR + 8192, [128, KVH * S], BF16)
    VTM = mem.at("v_tm", AR + 24576, [128, KVH * NTT * 256], BF16)
    CCS = mem.at("cc_sb", AR + 57344, [128, S], BF16)
    SSS = mem.at("ss_sb", AR + 61440, [128, S], BF16)
    T2S = mem.at("t2_sb", AR + 65536, [KVL, S], BF16)
    T1S = mem.at("t1_sb", AR + 69632, [QL, 512], BF16)
    VSB = mem.at("v_sb", AR + 70656, [HD, S], F32)
    MASK4 = mem.at("mask4", AR + 70656, [128, 4 * 512], BF16)
    WSTG = [mem.at(f"wstg{i}", AR + 65536 + 4096 * i, [128, QI], F32) for i in range(2)]
    RSCA = mem.at("ropeA", AR + 78848, [128, 512], BF16)
    RSCB = mem.at("ropeB", AR + 79872, [128, 512], BF16)
    PTS = mem.at("pt_sb", AR + 80896, [128, 4 * 1024], BF16)
    RLF = mem.at("rl_f32", AR + 89088, [128, 512], F32)
    RLB = mem.at("rl_bf", AR + 91136, [128, 512], BF16)
    BCB = mem.at("bc_bf", AR + 92160, [128, 1024], BF16)
    RSQ2A = mem.at("rope2A", AR + 89088, [128, 512], BF16)
    RSQ2B = mem.at("rope2B", AR + 91136, [128, 512], BF16)
    WOUT = mem.at("wout", AR + 0, [128, 8 * DM], BF16)
    XST2 = [mem.at(f"xst2_{i}", AR + 16384 + 4096 * i, [128, DM], F32) for i in range(2)]
    OUTST = [mem.at(f"outst_{i}", AR + 24576 + 4096 * i, [128, DM], F32) for i in range(2)]
    GSB = [mem.at(f"gsb_{i}", AR + 32768 + 4096 * i, [128, DM], F32) for i in range(2)]
    BOUTR = mem.at("bout_row", AR + 40960, [1, DM], F32)
    XST = [mem.at(f"xst_{i}", AR + 4096 * i, [128, DM], F32) for i in range(4)]
    XN32 = [mem.at(f"xn32_{i}", AR + 16384 + 4096 * i, [128, DM], F32) for i in range(2)]
    XNT = [mem.at(f"xnt_{i}", AR + 24576 + 4096 * i, [128, DM], F32) for i in range(2)]

    PSALL = nc.alloc_psum_tensor("psall", [128, 4096], F32)
    PS = PSALL.ap()

    p = Prog()
    q_const = p.sem("q_const")
    q_x = [p.sem(f"q_x{i}") for i in range(4)]
    q_st = [p.sem(f"q_st{i}") for i in range(2)]
    q_w1 = p.sem("q_w1")
    q_w2 = p.sem("q_w2")
    q_ws = [p.sem(f"q_ws{i}") for i in range(2)]
    q_sp = p.sem("q_sp")
    q_g = p.sem("q_g")
    q_wo = [p.sem(f"q_wo{i}") for i in range(8)]
    n_wo = [0] * 8
    s_cc = p.sem("s_cc")

    ncst = [0]

    def cdma(dst, src, waits=()):
        ncst[0] += 16
        p.dma("sync", lambda e, d=dst, s=src: e.dma_start(out=d, in_=s), waits, q_const)
        return (q_const, ncst[0])

    cdma(IDENT[:, :], ident_c.ap())
    cdma(ONESB[:, :], onesb_c.ap())
    cdma(ONESF[:, :], onesf_c.ap())
    cdma(ONESC[:, :], onesc_c.ap())
    cdma(BOUTR[0:1, :], bout_h.rearrange("(a d) -> a d", a=1))
    cdma(BGR[0:1, :], bg_h.rearrange("(a d) -> a d", a=1))
    for c in range(8):
        cdma(WG[:, c * NE:(c + 1) * NE], wg_h[c * 128:(c + 1) * 128, :])
    tok_const = cdma(WKVB32[:, :], wkvb_h)

    ngq = [0]

    def gdma(dst, src, waits=()):
        ngq[0] += 16
        p.dma("gpsimd", lambda e, d=dst, s=src: e.dma_start(out=d, in_=s), waits, q_g)
        return (q_g, ngq[0])

    for c in range(8):
        gdma(WQA[:, c * QL:(c + 1) * QL], wqa_h[c * 128:(c + 1) * 128, :])
    gdma(WQB[:, :], wqb_h)
    for c in range(8):
        gdma(WKVA[:, c * KVL:(c + 1) * KVL], wkva_h[c * 128:(c + 1) * 128, :])
    tok_gw = (None, 0)
    tok_gw = (_g_last_sem_tok(), 0) if False else (q_g, 0)
    tok_gw = (q_g, ngq[0])

    tok_zero = p.v(lambda e: e.memset(ZCOL[:, :], 0.0))

    wqb_v = WQB.ap().rearrange("p (i two) -> p i two", two=2)
    wqb2_v = WQB2.ap().rearrange("p (i two) -> p i two", two=2)
    p.v(lambda e: e.tensor_scalar(wqb2_v[:, :, 0], wqb_v[:, :, 1], -1.0, None, ALU.mult),
        waits=[tok_gw])
    p.v(lambda e: e.tensor_copy(wqb2_v[:, :, 1], wqb_v[:, :, 0]))
    kvb_v = WKVB32.ap().rearrange("p (j d two) -> p j d two", j=KVH, two=2)
    wkd_v = WKD.ap().rearrange("p (j h d) -> p j h d", j=KVH, h=2)
    for hh in range(2):
        p.v(lambda e, hh=hh: e.tensor_copy(wkd_v[:, :, hh, :], kvb_v[:, :, :, 0]),
            waits=[tok_const])
    kvb_p = WKVB32.ap().rearrange("p (j i four) -> p j i four", j=KVH, four=4)
    wkd2_p = WKD2.ap().rearrange("p (j h i two) -> p j h i two", j=KVH, h=2, two=2)
    for hh in range(2):
        p.v(lambda e, hh=hh: e.tensor_scalar(wkd2_p[:, :, hh, :, 0], kvb_p[:, :, :, 2], -1.0, None, ALU.mult))
        p.v(lambda e, hh=hh: e.tensor_copy(wkd2_p[:, :, hh, :, 1], kvb_p[:, :, :, 0]))
    wv_v = WV.ap().rearrange("p (j d) -> p j d", j=KVH)
    tok_smallw = p.v(lambda e: e.tensor_copy(wv_v[:, :, :], kvb_v[:, :, :, 1]))

    t_bb = None
    for n in range(2):
        t_bb = p.pe(lambda e, n=n: e.matmul(PS[:, 1536 + 512 * n:2048 + 512 * n],
                                            ONESF[0:1, :], BOUTR[0:1, 512 * n:512 * (n + 1)],
                                            start=True, stop=True),
                    waits=[tok_const])
    tok_boutb = p.v(lambda e: e.tensor_copy(BOUTB[:, :], PS[:, 1536:2560]), waits=[t_bb])

    tok_spillA = None
    xst_last = [None] * 4
    tb01_last = [None, None]
    lg_last = [None, None]
    xn32_rd = [None, None]
    xnt_rd = [None, None]
    p1_pe_last = p1_v_last = p1_g_last = None
    nx = [0] * 4
    xn_v = XN.ap().rearrange("p (c q) -> p c q", c=8)
    mt_v = MASKED.ap().rearrange("p (e t) -> p e t", e=NE)

    if "p1" in st:
        stage1 = {}

        def emit_stage1(g):
            s, t = g // NTT, g % NTT
            r, pb = g % 4, g % 2
            nx[r] += 16
            p.dma("sync",
                  lambda e, s=s, t=t, r=r: e.dma_start(out=XST[r][:, :], in_=x_h[s, t * TT:(t + 1) * TT, :]),
                  waits=[xst_last[r], tok_boutb if g == 0 else None], qsem=q_x[r])
            tok_x = (q_x[r], nx[r])
            t1 = p.v(lambda e, r=r, pb=pb: e.bn_stats(BNST[pb][:, 0:6], XST[r][:, 0:512]),
                     waits=[tok_x, stage1.get(g - 2)])
            t2 = p.v(lambda e, r=r, pb=pb: e.bn_stats(BNST[pb][:, 6:12], XST[r][:, 512:1024]), waits=[t1])
            t3 = p.v(lambda e, pb=pb: e.bn_aggr(MV[pb][:, :], BNST[pb][:, 0:12]), waits=[t2])
            t3b = p.v(lambda e, pb=pb: e.tensor_scalar(VEPS2[pb][:, :], MV[pb][:, 1:2], EPS_LN, None, ALU.add), waits=[t3])
            t4 = p.a(lambda e, pb=pb: e.activation(LNV[pb][:, :], VEPS2[pb][:, :], AF.Ln, bias=ZCOL[:, 0:1], scale=1.0),
                     waits=[t3b, tok_zero, stage1.get(g - 2)])
            t5 = p.a(lambda e, pb=pb: e.activation(RSTD[pb][:, :], LNV[pb][:, :], AF.Exp, bias=ZCOL[:, 0:1], scale=-0.5), waits=[t4])
            stage1[g] = t5

        emit_stage1(0)
        for s in range(SPC):
            for t in range(NTT):
                g = s * NTT + t
                r, xb, pb = g % 4, g % 2, g % 2
                if g + 1 < GT:
                    emit_stage1(g + 1)
                t5 = stage1[g]
                t6 = p.v(lambda e, pb=pb: e.tensor_scalar(NMR[pb][:, :], MV[pb][:, 0:1], RSTD[pb][:, 0:1], -1.0, ALU.mult, ALU.mult),
                         waits=[t5])
                t7 = p.v(lambda e, r=r, xb=xb, pb=pb: e.tensor_scalar(XN32[xb][:, :], XST[r][:, :], RSTD[pb][:, 0:1], NMR[pb][:, 0:1], ALU.mult, ALU.add),
                         waits=[t6, xn32_rd[xb]])
                xst_last[r] = t7
                tbk = [None, None]
                for c in range(8):
                    bk = c // 4
                    tbk[bk] = p.pe(lambda e, c=c, bk=bk, xb=xb: e.transpose(
                        PS[:, bk * 512 + (c % 4) * 128: bk * 512 + (c % 4) * 128 + 128],
                        XN32[xb][:, c * 128:(c + 1) * 128], IDENT[:, :]),
                        waits=[t7, tb01_last[bk], tok_const], sync=(c % 4 == 3))
                xn32_rd[xb] = tbk[1]
                ev = []
                for bk in range(2):
                    ev.append(p.v(lambda e, bk=bk, xb=xb: e.tensor_copy(
                        XNT[xb][:, bk * 512:(bk + 1) * 512], PS[:, bk * 512:(bk + 1) * 512]),
                        waits=[tbk[bk]] + (list(xnt_rd[xb][1:]) if xnt_rd[xb] else [])))
                tb01_last[0], tb01_last[1] = ev[0], ev[1]
                xnt_v = XNT[xb].ap().rearrange("p (c q) -> p c q", c=8)
                t8 = p.g(lambda e, t=t, xnt_v=xnt_v: e.tensor_copy(
                    xn_v[:, :, t * TT:(t + 1) * TT], xnt_v[:, :, :]),
                    waits=[ev[0], ev[1]] + ([tok_spillA] if (s == 1 and t == 0) else []))
                p1_g_last = t8
                lgs = 1024 + 16 * (g % 2)
                for c in range(8):
                    p.pe(lambda e, c=c, xb=xb, lgs=lgs: e.matmul(
                        PS[:, lgs:lgs + NE], XNT[xb][:, c * 128:(c + 1) * 128], WG[:, c * NE:(c + 1) * NE],
                        start=(c == 0), stop=False),
                        waits=([ev[0], ev[1], lg_last[g % 2]] if c == 0 else []), sync=False)
                mmb = p.pe(lambda e, lgs=lgs: e.matmul(PS[:, lgs:lgs + NE], ONESF[0:1, :], BGR[0:1, :],
                                                       start=False, stop=True))
                p1_pe_last = mmb
                xnt_rd[xb] = (t8[0], max(t8[1], 0)) if False else None
                xnt_rd[xb] = ("PAIR", mmb, t8)
                u1 = p.v(lambda e, lgs=lgs: e.reduce_max(NMX[:, :], PS[:, lgs:lgs + NE], axis=AX.X, negate=True),
                         waits=[mmb])
                u2 = p.a(lambda e, lgs=lgs: e.activation(PEXP[:, :], PS[:, lgs:lgs + NE], AF.Exp,
                                                         bias=NMX[:, 0:1], scale=1.0, accum_out=SEXP[:, 0:1]),
                         waits=[u1])
                lg_last[g % 2] = u2
                u3 = p.v(lambda e: e.reciprocal(RSUM[:, :], SEXP[:, 0:1]), waits=[u2])
                u4 = p.v(lambda e: e.reduce_max(PMX[:, :], PEXP[:, :], axis=AX.X), waits=[u2])
                u5 = p.v(lambda e: e.tensor_scalar(MSK8[:, :], PEXP[:, :], PMX[:, 0:1], None, ALU.is_ge),
                         waits=[u4])
                u6 = p.v(lambda e: e.tensor_mul(PNM[:, :], PEXP[:, :], MSK8[:, :]), waits=[u5])
                u7 = p.v(lambda e, g=g: e.tensor_scalar(mt_v[:, :, g], PNM[:, :], RSUM[:, 0:1], None, ALU.mult),
                         waits=[u6, u3])
                p1_v_last = u7
            if s == 0:
                p.dma("scalar", lambda e: e.dma_start(out=spillA, in_=XN[:, :]),
                      waits=[p1_g_last], qsem=q_sp)
                tok_spillA = (q_sp, 16)
        if "dbg_xn" in dbg:
            ncst[0] += 16
            p.dma("sync", lambda e: e.dma_start(out=dbg["dbg_xn"], in_=XN[:, :]),
                  waits=[p1_g_last], qsem=q_const)

    nsp = [16 if "p1" in st else 0]

    tok_gate = None
    tok_gatebc = None
    if "p2" in st:
        r1m = p.pe(lambda e: e.matmul(PS[:, 1536:1536 + NE * GT][0:1, :], ONESC[:, 0:1], MASKED[:, :],
                                      start=True, stop=True),
                   waits=[p1_v_last, tok_boutb, tok_const])
        r1 = p.v(lambda e: e.tensor_copy(MS1[0:1, :], PS[0:1, 1536:1536 + NE * GT]), waits=[r1m])
        ms1_v = MS1.ap().rearrange("p (e t) -> p e t", e=NE)
        r2 = p.v(lambda e: e.tensor_reduce(RROW[0:1, 0:NE], ms1_v[0:1, :, :], axis=AX.X, op=ALU.add),
                 waits=[r1])
        tok_ci = gdma(cc_in_d.ap(), RROW[0:1, :], waits=[r2])
        ccall = p.emit("gpsimd",
                       lambda e: e.collective_compute("AllReduce", ALU.add,
                                                      replica_groups=[list(range(n_cores))],
                                                      ins=[cc_in_d.ap().opt()],
                                                      outs=[cc_out_d.ap().opt()]),
                       waits=[tok_ci], inc=s_cc, n=1)
        ncst[0] += 16
        p.dma("sync", lambda e: e.dma_start(out=DENOM[0:1, :], in_=cc_out_d.ap()),
              waits=[ccall], qsem=q_const)
        tok_den = (q_const, ncst[0])
        d1 = p.v(lambda e: e.tensor_scalar(DENOM[0:1, :], DENOM[0:1, :], EPS_GATE, None, ALU.add),
                 waits=[tok_den])
        d2 = p.v(lambda e: e.reciprocal(RROW[0:1, :], DENOM[0:1, :]), waits=[d1])
        d3 = p.v(lambda e: e.tensor_scalar(RROW[0:1, :], RROW[0:1, :], T_TOTAL, None, ALU.mult), waits=[d2])
        bcm = p.pe(lambda e: e.matmul(PS[:, 1536:1536 + NE], ONESF[0:1, :], RROW[0:1, :],
                                      start=True, stop=True),
                   waits=[d3, tok_boutb])
        d4 = p.v(lambda e: e.tensor_copy(BCG[:, :], PS[:, 1536:1536 + NE]), waits=[bcm])
        tok_gate = None
        for ee in range(NE):
            tok_gate = p.v(lambda e, ee=ee: e.tensor_scalar(GATE[:, ee * GT:(ee + 1) * GT],
                                                            MASKED[:, ee * GT:(ee + 1) * GT],
                                                            BCG[:, ee:ee + 1], None, ALU.mult),
                           waits=[d4])
        tok_gatebc = d4
        if "dbg_gate" in dbg:
            ncst[0] += 16
            p.dma("sync", lambda e: e.dma_start(out=dbg["dbg_gate"], in_=GATE[:, :]),
                  waits=[tok_gate], qsem=q_const)

    nw1 = [0, 0]
    wst_cast = [None, None]
    nst = [0, 0]
    nx2 = [nx[0], nx[1]]
    w1_last_use = w2_last_use = None
    qk_last = vtm_last = att_last = bcm = None
    attn_t2_free = attn_t1_free = None
    exp_tok = [None, None]
    pv_tok = [None, None, None, None]
    yacc_guard = att_guard = None
    tok_reloadA = [None]
    yps_last = [None, None]
    hbk_last = [None] * 8
    st_last = [None, None]
    xn_last_use = None
    comb_pe_last = comb_v_last = None
    attn_pe_last = None
    gate_ready = tok_gate if tok_gate else p1_v_last

    for si, s in enumerate([1, 0]):
        if s == 1:
            tok_xn = p1_g_last
        elif tok_reloadA[0] is not None:
            tok_xn = tok_reloadA[0]
        elif st & {"ffn", "attn", "combine"}:
            nsp[0] += 16
            p.dma("scalar", lambda e: e.dma_start(out=XN[:, :], in_=spillA),
                  waits=[tok_spillA, xn_last_use], qsem=q_sp)
            tok_xn = (q_sp, nsp[0])
        else:
            tok_xn = None

        if "ffn" in st:
            if si == 0:
                gate_w1 = [p1_pe_last, p1_v_last, p1_g_last]
                gate_w2 = gate_w1
                gate_h = [tb01_last[0], tb01_last[1]]
            else:
                gate_w1 = [comb_pe_last, attn_pe_last]
                gate_w2 = [comb_v_last, st_last[0], st_last[1], attn_pe_last]
                gate_h = [comb_pe_last, comb_v_last, attn_pe_last]
            relu_last = None
            for e_i in range(NE):
                for qq in range(NQ):
                    wws = gate_w1 if (e_i == 0 and qq == 0) else []
                    wws2 = gate_w2 if (e_i == 0 and qq == 0) else []
                    first_q = (e_i == 0 and qq == 0)
                    for i in range(16):
                        sl = i % 2
                        nw1[sl] += 16
                        stg_w = [wst_cast[sl]]
                        if first_q:
                            stg_w += [attn_t2_free if sl == 0 else attn_t1_free]
                        if i < 8:
                            kc = i
                            p.dma("sync",
                                  lambda e, e_i=e_i, qq=qq, kc=kc, sl=sl: e.dma_start(
                                      out=WSTG[sl][:, :],
                                      in_=w1_h[e_i, kc * 128:(kc + 1) * 128, qq * QI:(qq + 1) * QI]),
                                  waits=stg_w, qsem=q_ws[sl])
                            ct = p.a(lambda e, kc=kc, sl=sl: e.copy(
                                W1S[:, kc * QI:(kc + 1) * QI], WSTG[sl][:, :]),
                                waits=[(q_ws[sl], nw1[sl]), w1_last_use if i == 0 else None] +
                                      (list(wws) if i == 0 else []))
                        else:
                            ic = i - 8
                            p.dma("sync",
                                  lambda e, e_i=e_i, qq=qq, ic=ic, sl=sl: e.dma_start(
                                      out=WSTG[sl][:, :],
                                      in_=w2_h[e_i, qq * QI + ic * 128:qq * QI + (ic + 1) * 128, :]),
                                  waits=stg_w, qsem=q_ws[sl])
                            ct = p.v(lambda e, ic=ic, sl=sl: e.tensor_copy(
                                W2S[:, ic * QI:(ic + 1) * QI], WSTG[sl][:, :]),
                                waits=[(q_ws[sl], nw1[sl]), w2_last_use if i == 8 else None] +
                                      (list(wws2) if i == 8 else []))
                        wst_cast[sl] = ct
                        if i == 7:
                            tok_w1 = ct
                        if i == 15:
                            tok_w2 = ct
                    rtok = {}
                    for ic in range(8):
                        par = (ic + 1) % 2
                        for tb in range(4):
                            bank = par * 4 + tb
                            mm = None
                            for kc in range(8):
                                extra = []
                                if kc == 0:
                                    extra = [hbk_last[bank] if hbk_last[bank] else
                                             (tok_gatebc if "p2" in st else None)]
                                    if tb == 0 and ic == 0:
                                        extra += [tok_w1, tok_xn]
                                    if ic in (0, 1) and tb == 0:
                                        extra += [yps_last[0], yps_last[1]]
                                mm = p.pe(lambda e, ic=ic, kc=kc, tb=tb, bank=bank: e.matmul(
                                    PS[:, bank * 512:(bank + 1) * 512],
                                    W1S[:, kc * QI + ic * 128:kc * QI + (ic + 1) * 128],
                                    XN[:, kc * 2048 + tb * 512:kc * 2048 + (tb + 1) * 512],
                                    start=(kc == 0), stop=(kc == 7)),
                                    waits=extra, sync=(kc == 7))
                            rt = p.a(lambda e, ic=ic, tb=tb, bank=bank: e.activation(
                                HSB[:, ic * 2048 + tb * 512:ic * 2048 + (tb + 1) * 512],
                                PS[:, bank * 512:(bank + 1) * 512], AF.Relu, bias=ZCOL[:, 0:1]),
                                waits=[mm] + (gate_h if (e_i == 0 and qq == 0 and ic == 0 and tb == 0) else []))
                            hbk_last[bank] = rt
                            rtok[(ic, tb)] = rt
                            relu_last = rt
                        w1_last_use = mm
                    for tt in range(NTT):
                        ypar = tt % 2
                        ybase = ypar * 1024
                        for nn in range(2):
                            mm = None
                            for ic in range(8):
                                extra = []
                                if nn == 0:
                                    extra = [rtok[(ic, tt // 4)]]
                                if ic == 0 and nn == 0:
                                    extra += [yps_last[ypar]]
                                    if tt == 0:
                                        extra += [tok_w2, rtok[(7, 0)], rtok[(7, 1)]]
                                    if tt == 1:
                                        extra += [rtok[(7, 2)], rtok[(7, 3)]]
                                mm = p.pe(lambda e, tt=tt, ic=ic, nn=nn, ybase=ybase: e.matmul(
                                    PS[:, ybase + nn * 512:ybase + (nn + 1) * 512],
                                    HSB[:, ic * 2048 + tt * 128:ic * 2048 + (tt + 1) * 128],
                                    W2S[:, ic * QI + nn * 512:ic * QI + (nn + 1) * 512],
                                    start=(ic == 0), stop=(ic == 7)),
                                    waits=extra, sync=(ic == 7 and nn == 1))
                        w2_last_use = mm
                        gt = s * NTT + tt
                        if e_i == 0 and qq == 0:
                            yt = p.v(lambda e, tt=tt, gt=gt, ybase=ybase: e.tensor_scalar(
                                YACC[:, tt * DM:(tt + 1) * DM], PS[:, ybase:ybase + 1024],
                                GATE[:, 0 * GT + gt:0 * GT + gt + 1], None, ALU.mult),
                                waits=[mm, gate_ready, yacc_guard])
                        else:
                            yt = p.v(lambda e, tt=tt, gt=gt, ybase=ybase, e_i=e_i: e.scalar_tensor_tensor(
                                YACC[:, tt * DM:(tt + 1) * DM], PS[:, ybase:ybase + 1024],
                                GATE[:, e_i * GT + gt:e_i * GT + gt + 1],
                                YACC[:, tt * DM:(tt + 1) * DM], ALU.mult, ALU.add),
                                waits=[mm, yps_last[ypar]])
                        yps_last[ypar] = yt
            tok_yacc = yps_last[(NTT - 1) % 2]
            if "dbg_yacc" in dbg and si == 0:
                ncst[0] += 16
                p.dma("sync", lambda e: e.dma_start(out=dbg["dbg_yacc"], in_=YACC[:, :]),
                      waits=[tok_yacc], qsem=q_const)
                yacc_guard = (q_const, ncst[0])
        else:
            tok_yacc = None

        if "attn" in st:
            ag = [w2_last_use, w1_last_use, yps_last[0], yps_last[1],
                  hbk_last[0], hbk_last[1], hbk_last[2], hbk_last[3],
                  hbk_last[4], hbk_last[5], hbk_last[6], hbk_last[7], tok_xn]
            if "ffn" not in st:
                ag = [tok_xn, gate_ready, p1_pe_last, p1_v_last, p1_g_last,
                      attn_pe_last, att_last, vtm_last]
            nsp[0] += 16
            p.dma("scalar", lambda e: e.dma_start(out=CCS[:, :], in_=cc_c.ap()),
                  waits=ag, qsem=q_sp)
            nsp[0] += 16
            p.dma("scalar", lambda e: e.dma_start(out=SSS[:, :], in_=ss_c.ap()), qsem=q_sp)
            tok_ccss = (q_sp, nsp[0])
            t2e = []
            for tb in range(4):
                mm = None
                for kc in range(8):
                    mm = p.pe(lambda e, tb=tb, kc=kc: e.matmul(
                        PS[0:KVL, 0:512], WKVA[:, kc * KVL:(kc + 1) * KVL],
                        XN[:, kc * 2048 + tb * 512:kc * 2048 + (tb + 1) * 512],
                        start=(kc == 0), stop=(kc == 7)),
                        waits=(ag + [t2e[-1] if t2e else None, tok_smallw]) if kc == 0 else [],
                        sync=(kc == 7))
                t2e.append(p.a(lambda e, tb=tb: e.activation(
                    T2S[0:KVL, tb * 512:(tb + 1) * 512], PS[0:KVL, 0:512], AF.Copy), waits=[mm]))
            stale = ([att_par_last[0], att_par_last[1], bc_par_last[0], bc_par_last[1]] +
                     list(exp_tok) + list(pv_tok)) \
                if si > 0 and "attn_done" in locals() else []
            z1 = p.v(lambda e: e.memset(VTM[:, :], 0.0), waits=ag + stale)
            vtm_ones = VTM.ap().rearrange("p (c w) -> p c w", c=KVH * NTT)
            z2 = p.v(lambda e: e.memset(vtm_ones[:, :, 64:65], 1.0), waits=[z1])
            tok_vz = p.v(lambda e: e.memset(vtm_ones[:, :, 160:161], 1.0), waits=[z2])
            KBA = [512, 1536]
            KBB = [1024, 3072]
            RS_A = [RSCA, RSQ2A]
            RS_B = [RSCB, RSQ2B]
            kd_toks = {}
            kd_last = None
            kpend = None

            def k_rope(i, j, tb, ss, mka, mkb):
                ra = p.v(lambda e, tb=tb, ss=ss: e.tensor_mul(RS_A[ss][:, :], PS[:, KBA[ss]:KBA[ss] + 512],
                                                              CCS[:, tb * 512:(tb + 1) * 512]),
                         waits=[mka, tok_ccss] + (stale if ss == 1 and i == 1 else []))
                rb = p.v(lambda e, tb=tb, ss=ss: e.tensor_mul(RS_B[ss][:, :], PS[:, KBB[ss]:KBB[ss] + 512],
                                                              SSS[:, tb * 512:(tb + 1) * 512]),
                         waits=[mkb])
                kd = p.v(lambda e, j=j, tb=tb, ss=ss: e.tensor_add(
                    K2X[:, j * S + tb * 512:j * S + (tb + 1) * 512], RS_A[ss][:, :], RS_B[ss][:, :]),
                    waits=[ra, rb])
                kd_toks[i] = kd
                return kd

            for i, (j, tb) in enumerate([(j, tb) for j in range(KVH) for tb in range(4)]):
                ss = i % 2
                mka = p.pe(lambda e, j=j, tb=tb, ss=ss: e.matmul(
                    PS[:, KBA[ss]:KBA[ss] + 512], WKD[:, j * 128:(j + 1) * 128],
                    T2S[0:KVL, tb * 512:(tb + 1) * 512], start=True, stop=True),
                    waits=[t2e[tb], kd_toks.get(i - 2)] + (ag + stale if i == 0 else []))
                mkb = p.pe(lambda e, j=j, tb=tb, ss=ss: e.matmul(
                    PS[:, KBB[ss]:KBB[ss] + 512], WKD2[:, j * 128:(j + 1) * 128],
                    T2S[0:KVL, tb * 512:(tb + 1) * 512], start=True, stop=True))
                if kpend is not None:
                    kd_last = k_rope(*kpend)
                kpend = (i, j, tb, ss, mka, mkb)
            kd_last = k_rope(*kpend)
            vtr_last = [None, None]
            vtm_last = None
            vsb_pe_last = None
            vbank_last = {0: t2e[3], 3584: None}
            for j in range(KVH):
                vsb_toks = []
                for tb in range(4):
                    vb = [0, 3584][tb % 2]
                    mv_ = p.pe(lambda e, j=j, tb=tb, vb=vb: e.matmul(
                        PS[0:HD, vb:vb + 512], WV[:, j * HD:(j + 1) * HD],
                        T2S[0:KVL, tb * 512:(tb + 1) * 512], start=True, stop=True),
                        waits=[t2e[tb], vbank_last[vb]] + (stale if j == 0 and tb <= 1 else []))
                    ev = p.v(lambda e, tb=tb, vb=vb: e.tensor_copy(
                        VSB[0:HD, tb * 512:(tb + 1) * 512], PS[0:HD, vb:vb + 512]), waits=[mv_])
                    vsb_toks.append(ev)
                    vbank_last[vb] = ev
                for c in range(NTT):
                    bk = 4 + (c % 2)
                    off = bk * 512 + ((c // 2) % 4) * 128
                    trv = p.pe(lambda e, c=c, off=off: e.transpose(
                        PS[:, off:off + HD], VSB[0:HD, c * 128:(c + 1) * 128], IDENT[0:HD, 0:HD]),
                        waits=[vsb_toks[c // 4], vtr_last[c % 2]])
                    if c == NTT - 1:
                        vsb_pe_last = trv
                    base = (j * NTT + c) * 256
                    p.v(lambda e, base=base, off=off: e.tensor_copy(
                        VTM[:, base:base + 64], PS[:, off:off + HD]),
                        waits=[trv, tok_vz])
                    ev2 = p.v(lambda e, base=base, off=off: e.tensor_copy(
                        VTM[:, base + 192:base + 256], PS[:, off:off + HD]))
                    vtr_last[c % 2] = ev2
                    vtm_last = ev2
            nsp[0] += 16
            p.dma("scalar", lambda e: e.dma_start(out=MASK4[:, :], in_=mask_c.ap()),
                  waits=[vsb_pe_last, wst_cast[0], wst_cast[1]], qsem=q_sp)
            tok_mask = (q_sp, nsp[0])
            attn_done = True
            att_par_last = [None, None]
            bc_par_last = [None, None]
            t1_evac_last = None
            qk_last = None
            att_last = None
            pend_tail = None
            last_bcm = [None]
            QBA = [512, 1536]
            QBB = [1024, 0]

            def make_tail(h, qb, o_tok, rc2):
                par = h % 2
                ob = 2048 + par * 512
                bb = 3072 + par * 512
                lr = 64 if par == 0 else 32
                ro = (0, 64) if par == 0 else (64, 128)

                def flush():
                    bcm = p.pe(lambda e, lr=lr, bb=bb: e.matmul(
                        PS[:, bb:bb + 512], ONESB[lr:lr + 1, :], RLB[lr:lr + 1, 0:512],
                        start=True, stop=True),
                        waits=[rc2, bc_par_last[par]])
                    last_bcm[0] = bcm
                    bce = p.v(lambda e, bb=bb, par=par: e.tensor_copy(
                        BCB[:, par * 512:par * 512 + 512], PS[:, bb:bb + 512]), waits=[bcm])
                    bc_par_last[par] = bce
                    at = p.v(lambda e, h=h, qb=qb, ro=ro, ob=ob, par=par: e.tensor_mul(
                        ATT[ro[0]:ro[1], (h // 2) * S + qb * 512:(h // 2) * S + (qb + 1) * 512],
                        PS[ro[0]:ro[1], ob:ob + 512], BCB[ro[0]:ro[1], par * 512:par * 512 + 512]),
                        waits=[bce] + ([att_guard] if (qb == 0 and h == 0) else []))
                    att_par_last[par] = at
                    return at
                return flush

            for qb in range(4):
                mm = None
                for kc in range(8):
                    mm = p.pe(lambda e, qb=qb, kc=kc: e.matmul(
                        PS[0:QL, 0:512], WQA[:, kc * QL:(kc + 1) * QL],
                        XN[:, kc * 2048 + qb * 512:kc * 2048 + (qb + 1) * 512],
                        start=(kc == 0), stop=(kc == 7)),
                        waits=[exp_tok[0], t1_evac_last, tok_gw,
                               pv_tok[0] if qb > 0 else None] if kc == 0 else [],
                        sync=(kc == 7))
                if qb == 3:
                    xn_last_use = mm
                    if si == 0 and "p1" in st and st & {"ffn", "attn", "combine"}:
                        ncst[0] += 16
                        p.dma("sync", lambda e: e.dma_start(out=XN[:, :], in_=spillA),
                              waits=[tok_spillA, mm], qsem=q_const)
                        tok_reloadA[0] = (q_const, ncst[0])
                if pend_tail is not None:
                    for _f in pend_tail:
                        att_last = _f()
                    pend_tail = None
                t1e = p.a(lambda e: e.activation(T1S[0:QL, 0:512], PS[0:QL, 0:512], AF.Copy),
                          waits=[mm])
                t1_evac_last = t1e
                qrope_last = None
                q_toks = {}
                qpend = None

                def q_rope(fc, ss, mqa, mqb, qb=qb):
                    ra = p.v(lambda e, ss=ss, qb=qb: e.tensor_mul(
                        RS_A[ss][:, :], PS[:, QBA[ss]:QBA[ss] + 512], CCS[:, qb * 512:(qb + 1) * 512]),
                        waits=[mqa] + ([att_par_last[0], att_par_last[1],
                                        bc_par_last[0], bc_par_last[1]] if ss == 1 and fc == 1 else []))
                    rb = p.v(lambda e, ss=ss, qb=qb: e.tensor_mul(
                        RS_B[ss][:, :], PS[:, QBB[ss]:QBB[ss] + 512], SSS[:, qb * 512:(qb + 1) * 512]),
                        waits=[mqb])
                    qd = p.v(lambda e, fc=fc, ss=ss: e.tensor_add(
                        QQB[:, fc * 512:(fc + 1) * 512], RS_A[ss][:, :], RS_B[ss][:, :]),
                        waits=[ra, rb, qk_last if fc <= 1 else None])
                    q_toks[fc] = qd
                    return qd

                for fc in range(8):
                    ss = fc % 2
                    mqa = p.pe(lambda e, fc=fc, ss=ss: e.matmul(
                        PS[:, QBA[ss]:QBA[ss] + 512], WQB[0:QL, fc * 128:(fc + 1) * 128], T1S[0:QL, 0:512],
                        start=True, stop=True),
                        waits=[t1e, q_toks.get(fc - 2),
                               exp_tok[0] if fc == 0 else None,
                               exp_tok[1] if fc == 0 else None])
                    mqb = p.pe(lambda e, fc=fc, ss=ss: e.matmul(
                        PS[:, QBB[ss]:QBB[ss] + 512], WQB2[0:QL, fc * 128:(fc + 1) * 128], T1S[0:QL, 0:512],
                        start=True, stop=True),
                        waits=[t1e if fc == 1 else None])
                    if qpend is not None:
                        q_rope(*qpend)
                    qpend = (fc, ss, mqa, mqb)
                qrope_last = q_rope(*qpend)
                for hp in range(0, NH, 2):
                    nch = 4 * (qb + 1)
                    groups = [list(range(gs, min(gs + 2, nch))) for gs in range(0, nch, 2)]
                    o_toks = [None, None]
                    pv_pend = None

                    def emit_pv(work):
                        for ho, grp, mk, j, pslot in work:
                            ob = 2048 + ho * 512
                            for idx, c in enumerate(grp):
                                o_toks[ho] = p.pe(lambda e, j=j, c=c, ho=ho, idx=idx, ob=ob, pslot=pslot, nch=nch: e.matmul(
                                    PS[:, ob:ob + 512],
                                    VTM[:, (j * NTT + c) * 256 + ho * 128:(j * NTT + c) * 256 + ho * 128 + 128],
                                    PTS[:, pslot + idx * 512:pslot + (idx + 1) * 512],
                                    start=(c == 0), stop=(c == nch - 1)),
                                    waits=([mk, vtm_last] if idx == 0 else []) +
                                          ([att_par_last[ho]] if (idx == 0 and c == 0) else []),
                                    sync=True)
                            pv_tok[(pslot // 1024)] = o_toks[ho]

                    for gi, grp in enumerate(groups):
                        work = []
                        for ho in range(2):
                            h = hp + ho
                            j, hb = h // 4, ho * 64
                            sb = ho * 1024
                            pslot = (ho * 2 + gi % 2) * 1024
                            fcq = h // 2
                            mm = None
                            for idx, c in enumerate(grp):
                                mm = p.pe(lambda e, j=j, hb=hb, c=c, sb=sb, idx=idx, fcq=fcq: e.matmul(
                                    PS[:, sb + idx * 512:sb + (idx + 1) * 512],
                                    K2X[hb:hb + 64, j * S + c * 128:j * S + (c + 1) * 128],
                                    QQB[hb:hb + 64, fcq * 512:fcq * 512 + 512],
                                    start=True, stop=True),
                                    waits=[exp_tok[ho]] + ([kd_last, qrope_last] if c == 0 else []),
                                    sync=(idx == len(grp) - 1))
                            qk_last = mm
                            if gi == 0 and pend_tail is not None:
                                att_last = pend_tail[ho]()
                                if ho == 1:
                                    pend_tail = None
                            ex = p.a(lambda e, pslot=pslot, sb=sb, n=len(grp): e.activation(
                                PTS[:, pslot:pslot + n * 512], PS[:, sb:sb + n * 512],
                                AF.Exp, bias=ZCOL[:, 0:1], scale=SCALE),
                                waits=[mm, pv_tok[ho * 2 + gi % 2]])
                            exp_tok[ho] = ex
                            mk = ex
                            for idx, c in enumerate(grp):
                                if c >= 4 * qb:
                                    r = c - 4 * qb
                                    mk = p.v(lambda e, pslot=pslot, idx=idx, r=r: e.tensor_mul(
                                        PTS[:, pslot + idx * 512:pslot + (idx + 1) * 512],
                                        PTS[:, pslot + idx * 512:pslot + (idx + 1) * 512],
                                        MASK4[:, r * 512:(r + 1) * 512]),
                                        waits=[ex, tok_mask, mk])
                            work.append((ho, grp, mk, j, pslot))
                        if pv_pend is not None:
                            emit_pv(pv_pend)
                        pv_pend = work
                    emit_pv(pv_pend)
                    tails = []
                    for ho in range(2):
                        h = hp + ho
                        ob = 2048 + ho * 512
                        lr = 64 if ho == 0 else 32
                        rc1 = p.a(lambda e, lr=lr, ob=ob: e.activation(
                            RLF[lr:lr + 1, 0:512], PS[lr:lr + 1, ob:ob + 512], AF.Ln,
                            bias=ZCOL[lr:lr + 1, 0:1], scale=1.0),
                            waits=[o_toks[ho], bc_par_last[ho],
                                   q_toks[7] if (qb == 0 or hp == 0) else None])
                        def _rc2(e, lr=lr):
                            with nc.allow_low_precision(reason="softmax 1/l as bf16 matmul operand"):
                                return e.activation(RLB[lr:lr + 1, 0:512], RLF[lr:lr + 1, 0:512], AF.Exp,
                                                    bias=ZCOL[lr:lr + 1, 0:1], scale=-1.0)
                        rc2 = p.a(_rc2, waits=[rc1])
                        tails.append(make_tail(h, qb, o_toks[ho], rc2))
                    pend_tail = tails
            for _f in pend_tail:
                att_last = _f()
            pend_tail = None
            bcm = last_bcm[0]
            attn_t2_free = vsb_pe_last
            attn_t1_free = xn_last_use
            attn_pe_last = bcm
            tok_att_done = att_last
            if "dbg_att" in dbg and si == 0:
                ncst[0] += 16
                p.dma("sync", lambda e: e.dma_start(out=dbg["dbg_att"], in_=ATT[:, :]),
                      waits=[tok_att_done], qsem=q_const)
                att_guard = (q_const, ncst[0])
        else:
            tok_att_done = None

        if "combine" in st:
            tok_wout = []
            for kc in range(8):
                n_wo[kc] += 16
                p.dma("gpsimd",
                      lambda e, kc=kc: e.dma_start(out=WOUT[:, kc * DM:(kc + 1) * DM],
                                                   in_=wout_h[kc * 128:(kc + 1) * 128, :]),
                      waits=[attn_pe_last, qk_last] if kc == 0 else (), qsem=q_wo[kc])
                tok_wout.append((q_wo[kc], n_wo[kc]))
            sps_last = [None, None]
            comb_x_last = [None, None]
            comb_g_last = [None, None]
            for tt in range(NTT):
                sl = tt % 2
                nx2[sl] += 16
                p.dma("sync",
                      lambda e, s=s, tt=tt, sl=sl: e.dma_start(out=XST2[sl][:, :],
                                                               in_=x_h[s, tt * TT:(tt + 1) * TT, :]),
                      waits=[comb_x_last[sl], attn_pe_last], qsem=q_x[sl])
                tok_x = (q_x[sl], nx2[sl])
                mm = None
                for kc in range(8):
                    for nn in range(2):
                        extra = []
                        if nn == 0:
                            extra = [tok_wout[kc]]
                        if kc == 0 and nn == 0:
                            extra += [sps_last[sl], tok_att_done]
                        mm = p.pe(lambda e, tt=tt, kc=kc, nn=nn, sl=sl: e.matmul(
                            PS[:, sl * 1024 + nn * 512:sl * 1024 + (nn + 1) * 512],
                            ATT[:, kc * S + tt * 128:kc * S + (tt + 1) * 128],
                            WOUT[:, kc * DM + nn * 512:kc * DM + (nn + 1) * 512],
                            start=(kc == 0), stop=(kc == 7)),
                            waits=extra, sync=(kc == 7 and nn == 1))
                comb_pe_last = mm
                gs = p.a(lambda e, tt=tt, sl=sl: e.activation(
                    GSB[sl][:, :], YACC[:, tt * DM:(tt + 1) * DM], AF.Silu, bias=ZCOL[:, 0:1]),
                    waits=[tok_yacc, comb_g_last[sl]] + ([attn_pe_last] if tt in (0, 1) else []))
                v1 = p.v(lambda e, sl=sl: e.tensor_add(OUTST[sl][:, :], PS[:, sl * 1024:sl * 1024 + 1024], BOUTB[:, :]),
                         waits=[mm, st_last[sl]] + ([vtm_last, att_last] if tt in (0, 1) else []))
                sps_last[sl] = v1
                v2 = p.v(lambda e, sl=sl: e.tensor_mul(OUTST[sl][:, :], OUTST[sl][:, :], GSB[sl][:, :]),
                         waits=[v1, gs])
                comb_g_last[sl] = v2
                v3 = p.v(lambda e, sl=sl: e.tensor_add(OUTST[sl][:, :], OUTST[sl][:, :], XST2[sl][:, :]),
                         waits=[v2, tok_x])
                comb_x_last[sl] = v3
                comb_v_last = v3
                nst[sl] += 16
                p.dma("sync",
                      lambda e, s=s, tt=tt, sl=sl: e.dma_start(out=out_h[s, tt * TT:(tt + 1) * TT, :],
                                                               in_=OUTST[sl][:, :]),
                      waits=[v3], qsem=q_st[sl])
                st_last[sl] = (q_st[sl], nst[sl])
            yps_last[0] = sps_last[0]
            yps_last[1] = sps_last[1]
            for _b in range(8):
                hbk_last[_b] = comb_g_last[1] if comb_g_last[1] else comb_g_last[0]

    p.emit("sync", lambda e: e.nop(), waits=[st_last[0], st_last[1],
                                             (q_st[0], nst[0]), (q_st[1], nst[1])])

    from contextlib import ExitStack
    with ExitStack() as stack:
        for sm in p.sems:
            if sm.n > 0 or sm is s_cc:
                sm.h = stack.enter_context(nc.semaphore(sm.name))
        for e in ENGINES:
            for sm, v in p.waited[e].items():
                assert sm.h is not None and v <= sm.n, (e, sm.name, v, sm.n)
        block = stack.enter_context(nc.Block())
        p.play(block)
    return nc


_NC_CACHE = {}


def _get_nc():
    if "nc" not in _NC_CACHE:
        _NC_CACHE["nc"] = build_program()
    return _NC_CACHE["nc"]


def kernel(L0_attn_x, L0_attn_Wqa, L0_attn_Wqb, L0_attn_Wkva, L0_attn_Wkvb,
           L0_attn_Wout, L0_attn_bout, L0_ffn_Wg, L0_ffn_bg, L0_ffn_W1, L0_ffn_W2):
    nc = _get_nc()
    x = np.ascontiguousarray(np.asarray(L0_attn_x, dtype=np.float32))
    shared = {
        "L0_attn_Wqa": np.ascontiguousarray(L0_attn_Wqa, dtype=np.float32),
        "L0_attn_Wqb": np.ascontiguousarray(L0_attn_Wqb, dtype=np.float32),
        "L0_attn_Wkva": np.ascontiguousarray(L0_attn_Wkva, dtype=np.float32),
        "L0_attn_Wkvb": np.ascontiguousarray(L0_attn_Wkvb, dtype=np.float32),
        "L0_attn_Wout": np.ascontiguousarray(L0_attn_Wout, dtype=np.float32),
        "L0_attn_bout": np.ascontiguousarray(L0_attn_bout, dtype=np.float32),
        "L0_ffn_Wg": np.ascontiguousarray(L0_ffn_Wg, dtype=np.float32),
        "L0_ffn_bg": np.ascontiguousarray(L0_ffn_bg, dtype=np.float32),
        "L0_ffn_W1": np.ascontiguousarray(L0_ffn_W1, dtype=np.float32),
        "L0_ffn_W2": np.ascontiguousarray(L0_ffn_W2, dtype=np.float32),
    }
    in_maps = []
    for c in range(NCORES):
        m = dict(shared)
        m["L0_attn_x"] = np.ascontiguousarray(x[c * SPC:(c + 1) * SPC])
        in_maps.append(m)
    res = run_bass_kernel_spmd(nc, in_maps, core_ids=list(range(NCORES)))
    return np.concatenate([r["out"] for r in res.results], axis=0)


# version 15
# speedup vs baseline: 1.0043x; 1.0043x over previous
import numpy as np
import concourse.bass as bass
import concourse.mybir as mybir
from concourse.bass_utils import run_bass_kernel_spmd

F32 = mybir.dt.float32
BF16 = mybir.dt.bfloat16
AF = mybir.ActivationFunctionType
ALU = mybir.AluOpType
AX = mybir.AxisListType
NPBF16 = mybir.dt.np(BF16)

B = 16; S = 2048; DM = 1024
NH = 16; KVH = 4; HD = 64; QL = 64; KVL = 32
NE = 8; FF = 4096
NCORES = 8; SPC = 2
NQ = 4; QI = 1024
TT = 128; NTT = S // TT; GT = SPC * NTT
EPS_LN = 1e-5; EPS_GATE = 1e-6
T_TOTAL = float(B * S)
SCALE = float(HD) ** -0.5

ENGINES = ("sync", "tensor", "vector", "scalar", "gpsimd")


class Sem:
    __slots__ = ("name", "n", "h")

    def __init__(self, name):
        self.name, self.n, self.h = name, 0, None


class Prog:
    def __init__(self):
        self.streams = {e: [] for e in ENGINES}
        self.waited = {e: {} for e in ENGINES}
        self.sems = []
        self.mono = {e: self.sem("m_" + e) for e in ENGINES}

    def sem(self, name):
        s = Sem(name)
        self.sems.append(s)
        return s

    def emit(self, eng, fn, waits=(), inc=None, n=1):
        ops, w = self.streams[eng], self.waited[eng]
        for t in waits:
            if t is None:
                continue
            sm, v = t
            if v > w.get(sm, 0):
                w[sm] = v
                ops.append(("w", sm, v))
        if inc is not None:
            inc.n += n
            ops.append(("o", fn, inc, n))
            return (inc, inc.n)
        ops.append(("o", fn, None, 0))
        return None

    def pe(self, fn, waits=(), sync=True):
        return self.emit("tensor", fn, waits, self.mono["tensor"] if sync else None)

    def v(self, fn, waits=(), sync=True):
        return self.emit("vector", fn, waits, self.mono["vector"] if sync else None)

    def a(self, fn, waits=(), sync=True):
        return self.emit("scalar", fn, waits, self.mono["scalar"] if sync else None)

    def g(self, fn, waits=(), sync=True):
        return self.emit("gpsimd", fn, waits, self.mono["gpsimd"] if sync else None)

    def dma(self, eng, fn, waits=(), qsem=None):
        return self.emit(eng, fn, waits, qsem, 16)

    def play(self, block):
        def body(name):
            def run(e):
                for it in self.streams[name]:
                    if it[0] == "w":
                        e.wait_ge(it[1].h, it[2])
                    else:
                        ins = it[1](e)
                        if it[2] is not None:
                            ins.then_inc(it[2].h, it[3])
            return run
        block.sync(body("sync"))
        block.tensor(body("tensor"))
        block.vector(body("vector"))
        block.scalar(body("scalar"))
        block.gpsimd(body("gpsimd"))


def _rope_tables():
    inv_freq = 1.0 / (10000.0 ** (np.arange(0, HD, 2, dtype=np.float64) / HD))
    p = np.arange(128)
    i = (p % HD) // 2
    ang = np.outer(inv_freq[i], np.arange(S, dtype=np.float64))
    return (np.cos(ang).astype(np.float32).astype(NPBF16),
            np.sin(ang).astype(np.float32).astype(NPBF16))


def _mask4():
    m = np.zeros((128, 4 * 512), dtype=np.float32)
    pp = np.arange(128)[:, None]
    qq = np.arange(512)[None, :]
    for r in range(4):
        m[:, r * 512:(r + 1) * 512] = (128 * r + pp <= qq).astype(np.float32)
    return m.astype(NPBF16)


class Mem:
    BASE = 16384 + 256

    def __init__(self, nc):
        self.nc, self.off = nc, self.BASE

    def at(self, name, off, shape, dtype):
        return self.nc.alloc_sbuf_tensor_at(name, list(shape), dtype, offset=off)

    def alloc(self, name, shape, dtype):
        bpe = 2 if dtype == BF16 else 4
        nby = (int(np.prod(shape[1:])) * bpe + 31) & ~31
        t = self.at(name, self.off, shape, dtype)
        self.off += nby
        return t


def build_program(n_cores=NCORES, stages=("p1", "p2", "ffn", "attn", "combine"),
                  debug=()):
    st = set(stages)
    nc = bass.Bass("TRN2", target_bir_lowering=False)

    x_h = nc.dram_tensor("L0_attn_x", [SPC, S, DM], F32, kind="ExternalInput").ap()
    wqa_h = nc.dram_tensor("L0_attn_Wqa", [DM, QL], F32, kind="ExternalInput").ap()
    wqb_h = nc.dram_tensor("L0_attn_Wqb", [QL, DM], F32, kind="ExternalInput").ap()
    wkva_h = nc.dram_tensor("L0_attn_Wkva", [DM, KVL], F32, kind="ExternalInput").ap()
    wkvb_h = nc.dram_tensor("L0_attn_Wkvb", [KVL, KVH * HD * 2], F32, kind="ExternalInput").ap()
    wout_h = nc.dram_tensor("L0_attn_Wout", [DM, DM], F32, kind="ExternalInput").ap()
    bout_h = nc.dram_tensor("L0_attn_bout", [DM], F32, kind="ExternalInput").ap()
    wg_h = nc.dram_tensor("L0_ffn_Wg", [DM, NE], F32, kind="ExternalInput").ap()
    bg_h = nc.dram_tensor("L0_ffn_bg", [NE], F32, kind="ExternalInput").ap()
    w1_h = nc.dram_tensor("L0_ffn_W1", [NE, DM, FF], F32, kind="ExternalInput").ap()
    w2_h = nc.dram_tensor("L0_ffn_W2", [NE, FF, DM], F32, kind="ExternalInput").ap()
    out_h = nc.dram_tensor("out", [SPC, S, DM], F32, kind="ExternalOutput").ap()

    cc_np, ss_np = _rope_tables()
    cc_c = nc.inline_tensor(cc_np, name="cc_const")
    ss_c = nc.inline_tensor(ss_np, name="ss_const")
    mask_c = nc.inline_tensor(_mask4(), name="mask_const")
    ident_c = nc.inline_tensor(np.eye(128, dtype=np.float32), name="ident_const")
    onesb_c = nc.inline_tensor(np.ones((128, 128), dtype=np.float32).astype(NPBF16), name="onesb_const")
    onesf_c = nc.inline_tensor(np.ones((1, 128), dtype=np.float32), name="onesf_const")
    onesc_c = nc.inline_tensor(np.ones((128, 1), dtype=np.float32), name="onesc_const")

    spillA = nc.dram_tensor("spillA", [128, 8 * S], BF16).ap()
    cc_in_d = nc.dram_tensor("cc_in", [1, NE], F32)
    cc_out_d = nc.dram_tensor("cc_out", [1, NE], F32)

    dbg = {}
    for name, shape, dt_ in (("dbg_xn", [128, 8 * S], BF16),
                             ("dbg_gate", [128, NE * GT], F32),
                             ("dbg_yacc", [128, NTT * DM], BF16),
                             ("dbg_att", [128, 8 * S], BF16)):
        if name in debug:
            dbg[name] = nc.dram_tensor(name, shape, dt_, kind="ExternalOutput").ap()

    mem = Mem(nc)
    XN = mem.alloc("xn_fm", [128, 8 * S], BF16)
    ATT = mem.alloc("attno", [128, 8 * S], BF16)
    YACC = mem.alloc("yacc", [128, NTT * DM], BF16)
    BOUTB = mem.alloc("bout_bc", [128, DM], F32)
    IDENT = mem.alloc("ident", [128, 128], F32)
    ONESB = mem.alloc("ones_bf", [128, 128], BF16)
    ONESF = mem.alloc("ones_f32", [1, 128], F32)
    ONESC = mem.alloc("ones_col", [128, 1], F32)
    WQA = mem.alloc("wqa", [128, 8 * QL], BF16)
    WQB = mem.alloc("wqb", [QL, DM], BF16)
    WQB2 = mem.alloc("wqb2", [QL, DM], BF16)
    WKVB32 = mem.alloc("wkvb32", [KVL, KVH * HD * 2], F32)
    WKD = mem.alloc("wkd", [KVL, KVH * 128], BF16)
    WKD2 = mem.alloc("wkd2", [KVL, KVH * 128], BF16)
    WV = mem.alloc("wv", [KVL, KVH * HD], BF16)
    WKVA = mem.alloc("wkva", [128, 8 * KVL], BF16)
    WG = mem.alloc("wg", [128, 8 * NE], F32)
    BGR = mem.alloc("bg_row", [1, NE], F32)
    MASKED = mem.alloc("masked_tm", [128, NE * GT], F32)
    GATE = mem.alloc("gate_tm", [128, NE * GT], F32)
    DENOM = mem.alloc("denom", [1, NE], F32)
    BCG = mem.alloc("bc_gate", [128, NE], F32)
    BNST = [mem.alloc(f"bnst{i}", [128, 12], F32) for i in range(2)]
    MV = [mem.alloc(f"mv{i}", [128, 2], F32) for i in range(2)]
    LNV = [mem.alloc(f"lnv{i}", [128, 1], F32) for i in range(2)]
    RSTD = [mem.alloc(f"rstd{i}", [128, 1], F32) for i in range(2)]
    NMR = [mem.alloc(f"nmr{i}", [128, 1], F32) for i in range(2)]
    VEPS2 = [mem.alloc(f"veps{i}", [128, 1], F32) for i in range(2)]
    NMX = mem.alloc("nmx", [128, 1], F32)
    SEXP = mem.alloc("sexp", [128, 1], F32)
    RSUM = mem.alloc("rsum", [128, 1], F32)
    VEPS = mem.alloc("veps", [128, 1], F32)
    ZCOL = mem.alloc("zcol", [128, 1], F32)
    PEXP = mem.alloc("pexp", [128, NE], F32)
    PMX = mem.alloc("pmx", [128, 1], F32)
    MSK8 = mem.alloc("msk8", [128, NE], F32)
    PNM = mem.alloc("pnm", [128, NE], F32)
    MS1 = mem.alloc("msum1", [1, NE * GT], F32)
    RROW = mem.alloc("rrow", [1, NE], F32)

    AR = (mem.off + 31) & ~31
    assert AR + 94208 <= 229200, f"SBUF over budget: {AR + 91136}"

    W1S = mem.at("w1s", AR + 0, [128, 8 * QI], BF16)
    W2S = mem.at("w2s", AR + 16384, [128, 8 * QI], BF16)
    HSB = mem.at("hsb", AR + 32768, [128, 8 * 2048], BF16)
    QQB = mem.at("q_qb", AR + 0, [128, 8 * 512], BF16)
    K2X = mem.at("k2x", AR + 8192, [128, KVH * S], BF16)
    VTM = mem.at("v_tm", AR + 24576, [128, KVH * NTT * 256], BF16)
    CCS = mem.at("cc_sb", AR + 57344, [128, S], BF16)
    SSS = mem.at("ss_sb", AR + 61440, [128, S], BF16)
    T2S = mem.at("t2_sb", AR + 65536, [KVL, S], BF16)
    T1S = mem.at("t1_sb", AR + 69632, [QL, 512], BF16)
    VSB = mem.at("v_sb", AR + 70656, [HD, S], F32)
    MASK4 = mem.at("mask4", AR + 70656, [128, 4 * 512], BF16)
    WSTG = [mem.at(f"wstg{i}", AR + 65536 + 4096 * i, [128, QI], F32) for i in range(2)]
    RSCA = mem.at("ropeA", AR + 78848, [128, 512], BF16)
    RSCB = mem.at("ropeB", AR + 79872, [128, 512], BF16)
    PTS = mem.at("pt_sb", AR + 80896, [128, 4 * 1024], BF16)
    RLF = mem.at("rl_f32", AR + 89088, [128, 512], F32)
    RLB = mem.at("rl_bf", AR + 91136, [128, 512], BF16)
    BCB = mem.at("bc_bf", AR + 92160, [128, 1024], BF16)
    RSQ2A = mem.at("rope2A", AR + 89088, [128, 512], BF16)
    RSQ2B = mem.at("rope2B", AR + 91136, [128, 512], BF16)
    WOUT = mem.at("wout", AR + 0, [128, 8 * DM], BF16)
    XST2 = [mem.at(f"xst2_{i}", AR + 16384 + 4096 * i, [128, DM], F32) for i in range(2)]
    OUTST = [mem.at(f"outst_{i}", AR + 24576 + 4096 * i, [128, DM], F32) for i in range(2)]
    GSB = [mem.at(f"gsb_{i}", AR + 32768 + 4096 * i, [128, DM], F32) for i in range(2)]
    BOUTR = mem.at("bout_row", AR + 40960, [1, DM], F32)
    XST = [mem.at(f"xst_{i}", AR + 4096 * i, [128, DM], F32) for i in range(4)]
    XN32 = [mem.at(f"xn32_{i}", AR + 16384 + 4096 * i, [128, DM], F32) for i in range(2)]
    XNT = [mem.at(f"xnt_{i}", AR + 24576 + 4096 * i, [128, DM], F32) for i in range(2)]

    PSALL = nc.alloc_psum_tensor("psall", [128, 4096], F32)
    PS = PSALL.ap()

    p = Prog()
    q_const = p.sem("q_const")
    q_x = [p.sem(f"q_x{i}") for i in range(4)]
    q_st = [p.sem(f"q_st{i}") for i in range(2)]
    q_w1 = p.sem("q_w1")
    q_w2 = p.sem("q_w2")
    q_ws = [p.sem(f"q_ws{i}") for i in range(2)]
    q_sp = p.sem("q_sp")
    q_g = p.sem("q_g")
    q_wo = [p.sem(f"q_wo{i}") for i in range(8)]
    n_wo = [0] * 8
    s_cc = p.sem("s_cc")

    ncst = [0]

    def cdma(dst, src, waits=()):
        ncst[0] += 16
        p.dma("sync", lambda e, d=dst, s=src: e.dma_start(out=d, in_=s), waits, q_const)
        return (q_const, ncst[0])

    cdma(IDENT[:, :], ident_c.ap())
    cdma(ONESB[:, :], onesb_c.ap())
    cdma(ONESF[:, :], onesf_c.ap())
    cdma(ONESC[:, :], onesc_c.ap())
    cdma(BOUTR[0:1, :], bout_h.rearrange("(a d) -> a d", a=1))
    cdma(BGR[0:1, :], bg_h.rearrange("(a d) -> a d", a=1))
    for c in range(8):
        cdma(WG[:, c * NE:(c + 1) * NE], wg_h[c * 128:(c + 1) * 128, :])
    tok_const = cdma(WKVB32[:, :], wkvb_h)

    ngq = [0]

    def gdma(dst, src, waits=()):
        ngq[0] += 16
        p.dma("gpsimd", lambda e, d=dst, s=src: e.dma_start(out=d, in_=s), waits, q_g)
        return (q_g, ngq[0])

    for c in range(8):
        gdma(WQA[:, c * QL:(c + 1) * QL], wqa_h[c * 128:(c + 1) * 128, :])
    gdma(WQB[:, :], wqb_h)
    for c in range(8):
        gdma(WKVA[:, c * KVL:(c + 1) * KVL], wkva_h[c * 128:(c + 1) * 128, :])
    tok_gw = (None, 0)
    tok_gw = (_g_last_sem_tok(), 0) if False else (q_g, 0)
    tok_gw = (q_g, ngq[0])

    tok_zero = p.v(lambda e: e.memset(ZCOL[:, :], 0.0))

    wqb_v = WQB.ap().rearrange("p (i two) -> p i two", two=2)
    wqb2_v = WQB2.ap().rearrange("p (i two) -> p i two", two=2)
    p.v(lambda e: e.tensor_scalar(wqb2_v[:, :, 0], wqb_v[:, :, 1], -1.0, None, ALU.mult),
        waits=[tok_gw])
    p.v(lambda e: e.tensor_copy(wqb2_v[:, :, 1], wqb_v[:, :, 0]))
    kvb_v = WKVB32.ap().rearrange("p (j d two) -> p j d two", j=KVH, two=2)
    wkd_v = WKD.ap().rearrange("p (j h d) -> p j h d", j=KVH, h=2)
    for hh in range(2):
        p.v(lambda e, hh=hh: e.tensor_copy(wkd_v[:, :, hh, :], kvb_v[:, :, :, 0]),
            waits=[tok_const])
    kvb_p = WKVB32.ap().rearrange("p (j i four) -> p j i four", j=KVH, four=4)
    wkd2_p = WKD2.ap().rearrange("p (j h i two) -> p j h i two", j=KVH, h=2, two=2)
    for hh in range(2):
        p.v(lambda e, hh=hh: e.tensor_scalar(wkd2_p[:, :, hh, :, 0], kvb_p[:, :, :, 2], -1.0, None, ALU.mult))
        p.v(lambda e, hh=hh: e.tensor_copy(wkd2_p[:, :, hh, :, 1], kvb_p[:, :, :, 0]))
    wv_v = WV.ap().rearrange("p (j d) -> p j d", j=KVH)
    tok_smallw = p.v(lambda e: e.tensor_copy(wv_v[:, :, :], kvb_v[:, :, :, 1]))

    t_bb = None
    for n in range(2):
        t_bb = p.pe(lambda e, n=n: e.matmul(PS[:, 1536 + 512 * n:2048 + 512 * n],
                                            ONESF[0:1, :], BOUTR[0:1, 512 * n:512 * (n + 1)],
                                            start=True, stop=True),
                    waits=[tok_const])
    tok_boutb = p.v(lambda e: e.tensor_copy(BOUTB[:, :], PS[:, 1536:2560]), waits=[t_bb])

    tok_spillA = None
    xst_last = [None] * 4
    tb01_last = [None, None]
    lg_last = [None, None]
    xn32_rd = [None, None]
    xnt_rd = [None, None]
    p1_pe_last = p1_v_last = p1_g_last = p1_g_last2 = None
    nx = [0] * 4
    xn_v = XN.ap().rearrange("p (c q) -> p c q", c=8)
    mt_v = MASKED.ap().rearrange("p (e t) -> p e t", e=NE)

    if "p1" in st:
        stage1 = {}

        def emit_stage1(g):
            s, t = g // NTT, g % NTT
            r, pb = g % 4, g % 2
            nx[r] += 16
            p.dma("sync",
                  lambda e, s=s, t=t, r=r: e.dma_start(out=XST[r][:, :], in_=x_h[s, t * TT:(t + 1) * TT, :]),
                  waits=[xst_last[r], tok_boutb if g == 0 else None], qsem=q_x[r])
            tok_x = (q_x[r], nx[r])
            t1 = p.v(lambda e, r=r, pb=pb: e.bn_stats(BNST[pb][:, 0:6], XST[r][:, 0:512]),
                     waits=[tok_x, stage1.get(g - 2)])
            t2 = p.v(lambda e, r=r, pb=pb: e.bn_stats(BNST[pb][:, 6:12], XST[r][:, 512:1024]), waits=[t1])
            t3 = p.v(lambda e, pb=pb: e.bn_aggr(MV[pb][:, :], BNST[pb][:, 0:12]), waits=[t2])
            t3b = p.v(lambda e, pb=pb: e.tensor_scalar(VEPS2[pb][:, :], MV[pb][:, 1:2], EPS_LN, None, ALU.add), waits=[t3])
            t4 = p.a(lambda e, pb=pb: e.activation(LNV[pb][:, :], VEPS2[pb][:, :], AF.Ln, bias=ZCOL[:, 0:1], scale=1.0),
                     waits=[t3b, tok_zero, stage1.get(g - 2)])
            t5 = p.a(lambda e, pb=pb: e.activation(RSTD[pb][:, :], LNV[pb][:, :], AF.Exp, bias=ZCOL[:, 0:1], scale=-0.5), waits=[t4])
            stage1[g] = t5

        emit_stage1(0)
        for s in range(SPC):
            for t in range(NTT):
                g = s * NTT + t
                r, xb, pb = g % 4, g % 2, g % 2
                if g + 1 < GT:
                    emit_stage1(g + 1)
                t5 = stage1[g]
                t6 = p.v(lambda e, pb=pb: e.tensor_scalar(NMR[pb][:, :], MV[pb][:, 0:1], RSTD[pb][:, 0:1], -1.0, ALU.mult, ALU.mult),
                         waits=[t5])
                t7 = p.v(lambda e, r=r, xb=xb, pb=pb: e.tensor_scalar(XN32[xb][:, :], XST[r][:, :], RSTD[pb][:, 0:1], NMR[pb][:, 0:1], ALU.mult, ALU.add),
                         waits=[t6, xn32_rd[xb]])
                xst_last[r] = t7
                tbk = [None, None]
                for c in range(8):
                    bk = c // 4
                    tbk[bk] = p.pe(lambda e, c=c, bk=bk, xb=xb: e.transpose(
                        PS[:, bk * 512 + (c % 4) * 128: bk * 512 + (c % 4) * 128 + 128],
                        XN32[xb][:, c * 128:(c + 1) * 128], IDENT[:, :]),
                        waits=[t7, tb01_last[bk], tok_const], sync=(c % 4 == 3))
                xn32_rd[xb] = tbk[1]
                ev = []
                for bk in range(2):
                    ev.append(p.a(lambda e, bk=bk, xb=xb: e.copy(
                        XNT[xb][:, bk * 512:(bk + 1) * 512], PS[:, bk * 512:(bk + 1) * 512]),
                        waits=[tbk[bk]] + (list(xnt_rd[xb][1:]) if xnt_rd[xb] else [])))
                tb01_last[0], tb01_last[1] = ev[0], ev[1]
                xnt_v = XNT[xb].ap().rearrange("p (c q) -> p c q", c=8)
                t8 = p.g(lambda e, t=t, xnt_v=xnt_v: e.tensor_copy(
                    xn_v[:, :, t * TT:(t + 1) * TT], xnt_v[:, :, :]),
                    waits=[ev[0], ev[1]] + ([tok_spillA] if (s == 1 and t == 0) else []))
                p1_g_last = t8
                lgs = 1024 + 16 * (g % 2)
                for c in range(8):
                    p.pe(lambda e, c=c, xb=xb, lgs=lgs: e.matmul(
                        PS[:, lgs:lgs + NE], XNT[xb][:, c * 128:(c + 1) * 128], WG[:, c * NE:(c + 1) * NE],
                        start=(c == 0), stop=False),
                        waits=([ev[0], ev[1], lg_last[g % 2]] if c == 0 else []), sync=False)
                mmb = p.pe(lambda e, lgs=lgs: e.matmul(PS[:, lgs:lgs + NE], ONESF[0:1, :], BGR[0:1, :],
                                                       start=False, stop=True))
                p1_pe_last = mmb
                xnt_rd[xb] = (t8[0], max(t8[1], 0)) if False else None
                xnt_rd[xb] = ("PAIR", mmb, t8)
                u1 = p.v(lambda e, lgs=lgs: e.reduce_max(NMX[:, :], PS[:, lgs:lgs + NE], axis=AX.X, negate=True),
                         waits=[mmb])
                u2 = p.a(lambda e, lgs=lgs: e.activation(PEXP[:, :], PS[:, lgs:lgs + NE], AF.Exp,
                                                         bias=NMX[:, 0:1], scale=1.0, accum_out=SEXP[:, 0:1]),
                         waits=[u1])
                lg_last[g % 2] = u2
                u3 = p.v(lambda e: e.reciprocal(RSUM[:, :], SEXP[:, 0:1]), waits=[u2])
                u4 = p.v(lambda e: e.reduce_max(PMX[:, :], PEXP[:, :], axis=AX.X), waits=[u2])
                u5 = p.v(lambda e: e.tensor_scalar(MSK8[:, :], PEXP[:, :], PMX[:, 0:1], None, ALU.is_ge),
                         waits=[u4])
                u6 = p.g(lambda e: e.tensor_mul(PNM[:, :], PEXP[:, :], MSK8[:, :]), waits=[u5])
                u7 = p.g(lambda e, g=g: e.tensor_scalar(mt_v[:, :, g], PNM[:, :], RSUM[:, 0:1], None, ALU.mult),
                         waits=[u6, u3])
                p1_v_last = u5
                p1_g_last2 = u7
            if s == 0:
                p.dma("scalar", lambda e: e.dma_start(out=spillA, in_=XN[:, :]),
                      waits=[p1_g_last], qsem=q_sp)
                tok_spillA = (q_sp, 16)
        if "dbg_xn" in dbg:
            ncst[0] += 16
            p.dma("sync", lambda e: e.dma_start(out=dbg["dbg_xn"], in_=XN[:, :]),
                  waits=[p1_g_last], qsem=q_const)

    nsp = [16 if "p1" in st else 0]

    tok_gate = None
    tok_gatebc = None
    if "p2" in st:
        r1m = p.pe(lambda e: e.matmul(PS[:, 1536:1536 + NE * GT][0:1, :], ONESC[:, 0:1], MASKED[:, :],
                                      start=True, stop=True),
                   waits=[p1_v_last, p1_g_last2, tok_boutb, tok_const])
        r1 = p.v(lambda e: e.tensor_copy(MS1[0:1, :], PS[0:1, 1536:1536 + NE * GT]), waits=[r1m])
        ms1_v = MS1.ap().rearrange("p (e t) -> p e t", e=NE)
        r2 = p.v(lambda e: e.tensor_reduce(RROW[0:1, 0:NE], ms1_v[0:1, :, :], axis=AX.X, op=ALU.add),
                 waits=[r1])
        tok_ci = gdma(cc_in_d.ap(), RROW[0:1, :], waits=[r2])
        ccall = p.emit("gpsimd",
                       lambda e: e.collective_compute("AllReduce", ALU.add,
                                                      replica_groups=[list(range(n_cores))],
                                                      ins=[cc_in_d.ap().opt()],
                                                      outs=[cc_out_d.ap().opt()]),
                       waits=[tok_ci], inc=s_cc, n=1)
        ncst[0] += 16
        p.dma("sync", lambda e: e.dma_start(out=DENOM[0:1, :], in_=cc_out_d.ap()),
              waits=[ccall], qsem=q_const)
        tok_den = (q_const, ncst[0])
        d1 = p.v(lambda e: e.tensor_scalar(DENOM[0:1, :], DENOM[0:1, :], EPS_GATE, None, ALU.add),
                 waits=[tok_den])
        d2 = p.v(lambda e: e.reciprocal(RROW[0:1, :], DENOM[0:1, :]), waits=[d1])
        d3 = p.v(lambda e: e.tensor_scalar(RROW[0:1, :], RROW[0:1, :], T_TOTAL, None, ALU.mult), waits=[d2])
        bcm = p.pe(lambda e: e.matmul(PS[:, 1536:1536 + NE], ONESF[0:1, :], RROW[0:1, :],
                                      start=True, stop=True),
                   waits=[d3, tok_boutb])
        d4 = p.v(lambda e: e.tensor_copy(BCG[:, :], PS[:, 1536:1536 + NE]), waits=[bcm])
        tok_gate = None
        for ee in range(NE):
            tok_gate = p.v(lambda e, ee=ee: e.tensor_scalar(GATE[:, ee * GT:(ee + 1) * GT],
                                                            MASKED[:, ee * GT:(ee + 1) * GT],
                                                            BCG[:, ee:ee + 1], None, ALU.mult),
                           waits=[d4, p1_g_last2])
        tok_gatebc = d4
        if "dbg_gate" in dbg:
            ncst[0] += 16
            p.dma("sync", lambda e: e.dma_start(out=dbg["dbg_gate"], in_=GATE[:, :]),
                  waits=[tok_gate], qsem=q_const)

    nw1 = [0, 0]
    wst_cast = [None, None]
    nst = [0, 0]
    nx2 = [nx[0], nx[1]]
    w1_last_use = w2_last_use = None
    qk_last = vtm_last = att_last = bcm = None
    attn_t2_free = attn_t1_free = None
    exp_tok = [None, None]
    pv_tok = [None, None, None, None]
    yacc_guard = att_guard = None
    tok_reloadA = [None]
    yps_last = [None, None]
    hbk_last = [None] * 8
    st_last = [None, None]
    xn_last_use = None
    comb_pe_last = comb_v_last = None
    attn_pe_last = None
    gate_ready = tok_gate if tok_gate else p1_v_last

    for si, s in enumerate([1, 0]):
        if s == 1:
            tok_xn = p1_g_last
        elif tok_reloadA[0] is not None:
            tok_xn = tok_reloadA[0]
        elif st & {"ffn", "attn", "combine"}:
            nsp[0] += 16
            p.dma("scalar", lambda e: e.dma_start(out=XN[:, :], in_=spillA),
                  waits=[tok_spillA, xn_last_use], qsem=q_sp)
            tok_xn = (q_sp, nsp[0])
        else:
            tok_xn = None

        if "ffn" in st:
            if si == 0:
                gate_w1 = [p1_pe_last, p1_v_last, p1_g_last]
                gate_w2 = gate_w1
                gate_h = [tb01_last[0], tb01_last[1]]
            else:
                gate_w1 = [comb_pe_last, attn_pe_last]
                gate_w2 = [comb_v_last, st_last[0], st_last[1], attn_pe_last]
                gate_h = [comb_pe_last, comb_v_last, attn_pe_last]
            relu_last = None
            for e_i in range(NE):
                for qq in range(NQ):
                    wws = gate_w1 if (e_i == 0 and qq == 0) else []
                    wws2 = gate_w2 if (e_i == 0 and qq == 0) else []
                    first_q = (e_i == 0 and qq == 0)
                    for i in range(16):
                        sl = i % 2
                        nw1[sl] += 16
                        stg_w = [wst_cast[sl]]
                        if first_q:
                            stg_w += [attn_t2_free if sl == 0 else attn_t1_free]
                        if i < 8:
                            kc = i
                            p.dma("sync",
                                  lambda e, e_i=e_i, qq=qq, kc=kc, sl=sl: e.dma_start(
                                      out=WSTG[sl][:, :],
                                      in_=w1_h[e_i, kc * 128:(kc + 1) * 128, qq * QI:(qq + 1) * QI]),
                                  waits=stg_w, qsem=q_ws[sl])
                            ct = p.a(lambda e, kc=kc, sl=sl: e.copy(
                                W1S[:, kc * QI:(kc + 1) * QI], WSTG[sl][:, :]),
                                waits=[(q_ws[sl], nw1[sl]), w1_last_use if i == 0 else None] +
                                      (list(wws) if i == 0 else []))
                        else:
                            ic = i - 8
                            p.dma("sync",
                                  lambda e, e_i=e_i, qq=qq, ic=ic, sl=sl: e.dma_start(
                                      out=WSTG[sl][:, :],
                                      in_=w2_h[e_i, qq * QI + ic * 128:qq * QI + (ic + 1) * 128, :]),
                                  waits=stg_w, qsem=q_ws[sl])
                            ct = p.v(lambda e, ic=ic, sl=sl: e.tensor_copy(
                                W2S[:, ic * QI:(ic + 1) * QI], WSTG[sl][:, :]),
                                waits=[(q_ws[sl], nw1[sl]), w2_last_use if i == 8 else None] +
                                      (list(wws2) if i == 8 else []))
                        wst_cast[sl] = ct
                        if i == 7:
                            tok_w1 = ct
                        if i == 15:
                            tok_w2 = ct
                    rtok = {}
                    for ic in range(8):
                        par = (ic + 1) % 2
                        for tb in range(4):
                            bank = par * 4 + tb
                            mm = None
                            for kc in range(8):
                                extra = []
                                if kc == 0:
                                    extra = [hbk_last[bank] if hbk_last[bank] else
                                             (tok_gatebc if "p2" in st else None)]
                                    if tb == 0 and ic == 0:
                                        extra += [tok_w1, tok_xn]
                                    if ic in (0, 1) and tb == 0:
                                        extra += [yps_last[0], yps_last[1]]
                                mm = p.pe(lambda e, ic=ic, kc=kc, tb=tb, bank=bank: e.matmul(
                                    PS[:, bank * 512:(bank + 1) * 512],
                                    W1S[:, kc * QI + ic * 128:kc * QI + (ic + 1) * 128],
                                    XN[:, kc * 2048 + tb * 512:kc * 2048 + (tb + 1) * 512],
                                    start=(kc == 0), stop=(kc == 7)),
                                    waits=extra, sync=(kc == 7))
                            rt = p.a(lambda e, ic=ic, tb=tb, bank=bank: e.activation(
                                HSB[:, ic * 2048 + tb * 512:ic * 2048 + (tb + 1) * 512],
                                PS[:, bank * 512:(bank + 1) * 512], AF.Relu, bias=ZCOL[:, 0:1]),
                                waits=[mm] + (gate_h if (e_i == 0 and qq == 0 and ic == 0 and tb == 0) else []))
                            hbk_last[bank] = rt
                            rtok[(ic, tb)] = rt
                            relu_last = rt
                        w1_last_use = mm
                    for tt in range(NTT):
                        ypar = tt % 2
                        ybase = ypar * 1024
                        for nn in range(2):
                            mm = None
                            for ic in range(8):
                                extra = []
                                if nn == 0:
                                    extra = [rtok[(ic, tt // 4)]]
                                if ic == 0 and nn == 0:
                                    extra += [yps_last[ypar]]
                                    if tt == 0:
                                        extra += [tok_w2, rtok[(7, 0)], rtok[(7, 1)]]
                                    if tt == 1:
                                        extra += [rtok[(7, 2)], rtok[(7, 3)]]
                                mm = p.pe(lambda e, tt=tt, ic=ic, nn=nn, ybase=ybase: e.matmul(
                                    PS[:, ybase + nn * 512:ybase + (nn + 1) * 512],
                                    HSB[:, ic * 2048 + tt * 128:ic * 2048 + (tt + 1) * 128],
                                    W2S[:, ic * QI + nn * 512:ic * QI + (nn + 1) * 512],
                                    start=(ic == 0), stop=(ic == 7)),
                                    waits=extra, sync=(ic == 7 and nn == 1))
                        w2_last_use = mm
                        gt = s * NTT + tt
                        if e_i == 0 and qq == 0:
                            yt = p.v(lambda e, tt=tt, gt=gt, ybase=ybase: e.tensor_scalar(
                                YACC[:, tt * DM:(tt + 1) * DM], PS[:, ybase:ybase + 1024],
                                GATE[:, 0 * GT + gt:0 * GT + gt + 1], None, ALU.mult),
                                waits=[mm, gate_ready, yacc_guard])
                        else:
                            yt = p.v(lambda e, tt=tt, gt=gt, ybase=ybase, e_i=e_i: e.scalar_tensor_tensor(
                                YACC[:, tt * DM:(tt + 1) * DM], PS[:, ybase:ybase + 1024],
                                GATE[:, e_i * GT + gt:e_i * GT + gt + 1],
                                YACC[:, tt * DM:(tt + 1) * DM], ALU.mult, ALU.add),
                                waits=[mm, yps_last[ypar]])
                        yps_last[ypar] = yt
            tok_yacc = yps_last[(NTT - 1) % 2]
            if "dbg_yacc" in dbg and si == 0:
                ncst[0] += 16
                p.dma("sync", lambda e: e.dma_start(out=dbg["dbg_yacc"], in_=YACC[:, :]),
                      waits=[tok_yacc], qsem=q_const)
                yacc_guard = (q_const, ncst[0])
        else:
            tok_yacc = None

        if "attn" in st:
            ag = [w2_last_use, w1_last_use, yps_last[0], yps_last[1],
                  hbk_last[0], hbk_last[1], hbk_last[2], hbk_last[3],
                  hbk_last[4], hbk_last[5], hbk_last[6], hbk_last[7], tok_xn]
            if "ffn" not in st:
                ag = [tok_xn, gate_ready, p1_pe_last, p1_v_last, p1_g_last,
                      attn_pe_last, att_last, vtm_last]
            nsp[0] += 16
            p.dma("scalar", lambda e: e.dma_start(out=CCS[:, :], in_=cc_c.ap()),
                  waits=ag, qsem=q_sp)
            nsp[0] += 16
            p.dma("scalar", lambda e: e.dma_start(out=SSS[:, :], in_=ss_c.ap()), qsem=q_sp)
            tok_ccss = (q_sp, nsp[0])
            t2e = []
            for tb in range(4):
                mm = None
                for kc in range(8):
                    mm = p.pe(lambda e, tb=tb, kc=kc: e.matmul(
                        PS[0:KVL, 0:512], WKVA[:, kc * KVL:(kc + 1) * KVL],
                        XN[:, kc * 2048 + tb * 512:kc * 2048 + (tb + 1) * 512],
                        start=(kc == 0), stop=(kc == 7)),
                        waits=(ag + [t2e[-1] if t2e else None, tok_smallw]) if kc == 0 else [],
                        sync=(kc == 7))
                t2e.append(p.a(lambda e, tb=tb: e.activation(
                    T2S[0:KVL, tb * 512:(tb + 1) * 512], PS[0:KVL, 0:512], AF.Copy), waits=[mm]))
            stale = ([att_par_last[0], att_par_last[1], bc_par_last[0], bc_par_last[1]] +
                     list(exp_tok) + list(pv_tok)) \
                if si > 0 and "attn_done" in locals() else []
            z1 = p.v(lambda e: e.memset(VTM[:, :], 0.0), waits=ag + stale)
            vtm_ones = VTM.ap().rearrange("p (c w) -> p c w", c=KVH * NTT)
            z2 = p.v(lambda e: e.memset(vtm_ones[:, :, 64:65], 1.0), waits=[z1])
            tok_vz = p.v(lambda e: e.memset(vtm_ones[:, :, 160:161], 1.0), waits=[z2])
            KBA = [512, 1536]
            KBB = [1024, 3072]
            RS_A = [RSCA, RSQ2A]
            RS_B = [RSCB, RSQ2B]
            kd_toks = {}
            kd_last = None
            kpend = None

            def k_rope(i, j, tb, ss, mka, mkb):
                ra = p.v(lambda e, tb=tb, ss=ss: e.tensor_mul(RS_A[ss][:, :], PS[:, KBA[ss]:KBA[ss] + 512],
                                                              CCS[:, tb * 512:(tb + 1) * 512]),
                         waits=[mka, tok_ccss] + (stale if ss == 1 and i == 1 else []))
                rb = p.v(lambda e, tb=tb, ss=ss: e.tensor_mul(RS_B[ss][:, :], PS[:, KBB[ss]:KBB[ss] + 512],
                                                              SSS[:, tb * 512:(tb + 1) * 512]),
                         waits=[mkb])
                kd = p.v(lambda e, j=j, tb=tb, ss=ss: e.tensor_add(
                    K2X[:, j * S + tb * 512:j * S + (tb + 1) * 512], RS_A[ss][:, :], RS_B[ss][:, :]),
                    waits=[ra, rb])
                kd_toks[i] = kd
                return kd

            for i, (j, tb) in enumerate([(j, tb) for j in range(KVH) for tb in range(4)]):
                ss = i % 2
                mka = p.pe(lambda e, j=j, tb=tb, ss=ss: e.matmul(
                    PS[:, KBA[ss]:KBA[ss] + 512], WKD[:, j * 128:(j + 1) * 128],
                    T2S[0:KVL, tb * 512:(tb + 1) * 512], start=True, stop=True),
                    waits=[t2e[tb], kd_toks.get(i - 2)] + (ag + stale if i == 0 else []))
                mkb = p.pe(lambda e, j=j, tb=tb, ss=ss: e.matmul(
                    PS[:, KBB[ss]:KBB[ss] + 512], WKD2[:, j * 128:(j + 1) * 128],
                    T2S[0:KVL, tb * 512:(tb + 1) * 512], start=True, stop=True))
                if kpend is not None:
                    kd_last = k_rope(*kpend)
                kpend = (i, j, tb, ss, mka, mkb)
            kd_last = k_rope(*kpend)
            vtr_last = [None, None]
            vtm_last = None
            vsb_pe_last = None
            vbank_last = {0: t2e[3], 3584: None}
            for j in range(KVH):
                vsb_toks = []
                for tb in range(4):
                    vb = [0, 3584][tb % 2]
                    mv_ = p.pe(lambda e, j=j, tb=tb, vb=vb: e.matmul(
                        PS[0:HD, vb:vb + 512], WV[:, j * HD:(j + 1) * HD],
                        T2S[0:KVL, tb * 512:(tb + 1) * 512], start=True, stop=True),
                        waits=[t2e[tb], vbank_last[vb]] + (stale if j == 0 and tb <= 1 else []))
                    ev = p.v(lambda e, tb=tb, vb=vb: e.tensor_copy(
                        VSB[0:HD, tb * 512:(tb + 1) * 512], PS[0:HD, vb:vb + 512]), waits=[mv_])
                    vsb_toks.append(ev)
                    vbank_last[vb] = ev
                for c in range(NTT):
                    bk = 4 + (c % 2)
                    off = bk * 512 + ((c // 2) % 4) * 128
                    trv = p.pe(lambda e, c=c, off=off: e.transpose(
                        PS[:, off:off + HD], VSB[0:HD, c * 128:(c + 1) * 128], IDENT[0:HD, 0:HD]),
                        waits=[vsb_toks[c // 4], vtr_last[c % 2]])
                    if c == NTT - 1:
                        vsb_pe_last = trv
                    base = (j * NTT + c) * 256
                    p.v(lambda e, base=base, off=off: e.tensor_copy(
                        VTM[:, base:base + 64], PS[:, off:off + HD]),
                        waits=[trv, tok_vz])
                    ev2 = p.v(lambda e, base=base, off=off: e.tensor_copy(
                        VTM[:, base + 192:base + 256], PS[:, off:off + HD]))
                    vtr_last[c % 2] = ev2
                    vtm_last = ev2
            nsp[0] += 16
            p.dma("scalar", lambda e: e.dma_start(out=MASK4[:, :], in_=mask_c.ap()),
                  waits=[vsb_pe_last, wst_cast[0], wst_cast[1]], qsem=q_sp)
            tok_mask = (q_sp, nsp[0])
            attn_done = True
            att_par_last = [None, None]
            bc_par_last = [None, None]
            t1_evac_last = None
            qk_last = None
            att_last = None
            pend_tail = None
            last_bcm = [None]
            QBA = [512, 1536]
            QBB = [1024, 0]

            def make_tail(h, qb, o_tok, rc2):
                par = h % 2
                ob = 2048 + par * 512
                bb = 3072 + par * 512
                lr = 64 if par == 0 else 32
                ro = (0, 64) if par == 0 else (64, 128)

                def flush():
                    bcm = p.pe(lambda e, lr=lr, bb=bb: e.matmul(
                        PS[:, bb:bb + 512], ONESB[lr:lr + 1, :], RLB[lr:lr + 1, 0:512],
                        start=True, stop=True),
                        waits=[rc2, bc_par_last[par]])
                    last_bcm[0] = bcm
                    bce = p.v(lambda e, bb=bb, par=par: e.tensor_copy(
                        BCB[:, par * 512:par * 512 + 512], PS[:, bb:bb + 512]), waits=[bcm])
                    bc_par_last[par] = bce
                    at = p.v(lambda e, h=h, qb=qb, ro=ro, ob=ob, par=par: e.tensor_mul(
                        ATT[ro[0]:ro[1], (h // 2) * S + qb * 512:(h // 2) * S + (qb + 1) * 512],
                        PS[ro[0]:ro[1], ob:ob + 512], BCB[ro[0]:ro[1], par * 512:par * 512 + 512]),
                        waits=[bce] + ([att_guard] if (qb == 0 and h == 0) else []))
                    att_par_last[par] = at
                    return at
                return flush

            for qb in range(4):
                mm = None
                for kc in range(8):
                    mm = p.pe(lambda e, qb=qb, kc=kc: e.matmul(
                        PS[0:QL, 0:512], WQA[:, kc * QL:(kc + 1) * QL],
                        XN[:, kc * 2048 + qb * 512:kc * 2048 + (qb + 1) * 512],
                        start=(kc == 0), stop=(kc == 7)),
                        waits=[exp_tok[0], t1_evac_last, tok_gw,
                               pv_tok[0] if qb > 0 else None] if kc == 0 else [],
                        sync=(kc == 7))
                if qb == 3:
                    xn_last_use = mm
                    if si == 0 and "p1" in st and st & {"ffn", "attn", "combine"}:
                        ncst[0] += 16
                        p.dma("sync", lambda e: e.dma_start(out=XN[:, :], in_=spillA),
                              waits=[tok_spillA, mm], qsem=q_const)
                        tok_reloadA[0] = (q_const, ncst[0])
                if pend_tail is not None:
                    for _f in pend_tail:
                        att_last = _f()
                    pend_tail = None
                t1e = p.a(lambda e: e.activation(T1S[0:QL, 0:512], PS[0:QL, 0:512], AF.Copy),
                          waits=[mm])
                t1_evac_last = t1e
                qrope_last = None
                q_toks = {}
                qpend = None

                def q_rope(fc, ss, mqa, mqb, qb=qb):
                    ra = p.v(lambda e, ss=ss, qb=qb: e.tensor_mul(
                        RS_A[ss][:, :], PS[:, QBA[ss]:QBA[ss] + 512], CCS[:, qb * 512:(qb + 1) * 512]),
                        waits=[mqa] + ([att_par_last[0], att_par_last[1],
                                        bc_par_last[0], bc_par_last[1]] if ss == 1 and fc == 1 else []))
                    rb = p.v(lambda e, ss=ss, qb=qb: e.tensor_mul(
                        RS_B[ss][:, :], PS[:, QBB[ss]:QBB[ss] + 512], SSS[:, qb * 512:(qb + 1) * 512]),
                        waits=[mqb])
                    qd = p.v(lambda e, fc=fc, ss=ss: e.tensor_add(
                        QQB[:, fc * 512:(fc + 1) * 512], RS_A[ss][:, :], RS_B[ss][:, :]),
                        waits=[ra, rb, qk_last if fc <= 1 else None])
                    q_toks[fc] = qd
                    return qd

                for fc in range(8):
                    ss = fc % 2
                    mqa = p.pe(lambda e, fc=fc, ss=ss: e.matmul(
                        PS[:, QBA[ss]:QBA[ss] + 512], WQB[0:QL, fc * 128:(fc + 1) * 128], T1S[0:QL, 0:512],
                        start=True, stop=True),
                        waits=[t1e, q_toks.get(fc - 2),
                               exp_tok[0] if fc == 0 else None,
                               exp_tok[1] if fc == 0 else None])
                    mqb = p.pe(lambda e, fc=fc, ss=ss: e.matmul(
                        PS[:, QBB[ss]:QBB[ss] + 512], WQB2[0:QL, fc * 128:(fc + 1) * 128], T1S[0:QL, 0:512],
                        start=True, stop=True),
                        waits=[t1e if fc == 1 else None])
                    if qpend is not None:
                        q_rope(*qpend)
                    qpend = (fc, ss, mqa, mqb)
                qrope_last = q_rope(*qpend)
                for hp in range(0, NH, 2):
                    nch = 4 * (qb + 1)
                    groups = [list(range(gs, min(gs + 2, nch))) for gs in range(0, nch, 2)]
                    o_toks = [None, None]
                    pv_pend = None

                    def emit_pv(work):
                        for ho, grp, mk, j, pslot in work:
                            ob = 2048 + ho * 512
                            for idx, c in enumerate(grp):
                                o_toks[ho] = p.pe(lambda e, j=j, c=c, ho=ho, idx=idx, ob=ob, pslot=pslot, nch=nch: e.matmul(
                                    PS[:, ob:ob + 512],
                                    VTM[:, (j * NTT + c) * 256 + ho * 128:(j * NTT + c) * 256 + ho * 128 + 128],
                                    PTS[:, pslot + idx * 512:pslot + (idx + 1) * 512],
                                    start=(c == 0), stop=(c == nch - 1)),
                                    waits=([mk, vtm_last] if idx == 0 else []) +
                                          ([att_par_last[ho]] if (idx == 0 and c == 0) else []),
                                    sync=True)
                            pv_tok[(pslot // 1024)] = o_toks[ho]

                    for gi, grp in enumerate(groups):
                        work = []
                        for ho in range(2):
                            h = hp + ho
                            j, hb = h // 4, ho * 64
                            sb = ho * 1024
                            pslot = (ho * 2 + gi % 2) * 1024
                            fcq = h // 2
                            mm = None
                            for idx, c in enumerate(grp):
                                mm = p.pe(lambda e, j=j, hb=hb, c=c, sb=sb, idx=idx, fcq=fcq: e.matmul(
                                    PS[:, sb + idx * 512:sb + (idx + 1) * 512],
                                    K2X[hb:hb + 64, j * S + c * 128:j * S + (c + 1) * 128],
                                    QQB[hb:hb + 64, fcq * 512:fcq * 512 + 512],
                                    start=True, stop=True),
                                    waits=[exp_tok[ho]] + ([kd_last, qrope_last] if c == 0 else []),
                                    sync=(idx == len(grp) - 1))
                            qk_last = mm
                            if gi == 0 and pend_tail is not None:
                                att_last = pend_tail[ho]()
                                if ho == 1:
                                    pend_tail = None
                            ex = p.a(lambda e, pslot=pslot, sb=sb, n=len(grp): e.activation(
                                PTS[:, pslot:pslot + n * 512], PS[:, sb:sb + n * 512],
                                AF.Exp, bias=ZCOL[:, 0:1], scale=SCALE),
                                waits=[mm, pv_tok[ho * 2 + gi % 2]])
                            exp_tok[ho] = ex
                            mk = ex
                            for idx, c in enumerate(grp):
                                if c >= 4 * qb:
                                    r = c - 4 * qb
                                    mk = p.v(lambda e, pslot=pslot, idx=idx, r=r: e.tensor_mul(
                                        PTS[:, pslot + idx * 512:pslot + (idx + 1) * 512],
                                        PTS[:, pslot + idx * 512:pslot + (idx + 1) * 512],
                                        MASK4[:, r * 512:(r + 1) * 512]),
                                        waits=[ex, tok_mask, mk])
                            work.append((ho, grp, mk, j, pslot))
                        if pv_pend is not None:
                            emit_pv(pv_pend)
                        pv_pend = work
                    emit_pv(pv_pend)
                    tails = []
                    for ho in range(2):
                        h = hp + ho
                        ob = 2048 + ho * 512
                        lr = 64 if ho == 0 else 32
                        rc1 = p.a(lambda e, lr=lr, ob=ob: e.activation(
                            RLF[lr:lr + 1, 0:512], PS[lr:lr + 1, ob:ob + 512], AF.Ln,
                            bias=ZCOL[lr:lr + 1, 0:1], scale=1.0),
                            waits=[o_toks[ho], bc_par_last[ho],
                                   q_toks[7] if (qb == 0 or hp == 0) else None])
                        def _rc2(e, lr=lr):
                            with nc.allow_low_precision(reason="softmax 1/l as bf16 matmul operand"):
                                return e.activation(RLB[lr:lr + 1, 0:512], RLF[lr:lr + 1, 0:512], AF.Exp,
                                                    bias=ZCOL[lr:lr + 1, 0:1], scale=-1.0)
                        rc2 = p.a(_rc2, waits=[rc1])
                        tails.append(make_tail(h, qb, o_toks[ho], rc2))
                    pend_tail = tails
            for _f in pend_tail:
                att_last = _f()
            pend_tail = None
            bcm = last_bcm[0]
            attn_t2_free = vsb_pe_last
            attn_t1_free = xn_last_use
            attn_pe_last = bcm
            tok_att_done = att_last
            if "dbg_att" in dbg and si == 0:
                ncst[0] += 16
                p.dma("sync", lambda e: e.dma_start(out=dbg["dbg_att"], in_=ATT[:, :]),
                      waits=[tok_att_done], qsem=q_const)
                att_guard = (q_const, ncst[0])
        else:
            tok_att_done = None

        if "combine" in st:
            tok_wout = []
            for kc in range(8):
                n_wo[kc] += 16
                p.dma("gpsimd",
                      lambda e, kc=kc: e.dma_start(out=WOUT[:, kc * DM:(kc + 1) * DM],
                                                   in_=wout_h[kc * 128:(kc + 1) * 128, :]),
                      waits=[attn_pe_last, qk_last] if kc == 0 else (), qsem=q_wo[kc])
                tok_wout.append((q_wo[kc], n_wo[kc]))
            sps_last = [None, None]
            comb_x_last = [None, None]
            comb_g_last = [None, None]
            for tt in range(NTT):
                sl = tt % 2
                nx2[sl] += 16
                p.dma("sync",
                      lambda e, s=s, tt=tt, sl=sl: e.dma_start(out=XST2[sl][:, :],
                                                               in_=x_h[s, tt * TT:(tt + 1) * TT, :]),
                      waits=[comb_x_last[sl], attn_pe_last], qsem=q_x[sl])
                tok_x = (q_x[sl], nx2[sl])
                mm = None
                for kc in range(8):
                    for nn in range(2):
                        extra = []
                        if nn == 0:
                            extra = [tok_wout[kc]]
                        if kc == 0 and nn == 0:
                            extra += [sps_last[sl], tok_att_done]
                        mm = p.pe(lambda e, tt=tt, kc=kc, nn=nn, sl=sl: e.matmul(
                            PS[:, sl * 1024 + nn * 512:sl * 1024 + (nn + 1) * 512],
                            ATT[:, kc * S + tt * 128:kc * S + (tt + 1) * 128],
                            WOUT[:, kc * DM + nn * 512:kc * DM + (nn + 1) * 512],
                            start=(kc == 0), stop=(kc == 7)),
                            waits=extra, sync=(kc == 7 and nn == 1))
                comb_pe_last = mm
                gs = p.a(lambda e, tt=tt, sl=sl: e.activation(
                    GSB[sl][:, :], YACC[:, tt * DM:(tt + 1) * DM], AF.Silu, bias=ZCOL[:, 0:1]),
                    waits=[tok_yacc, comb_g_last[sl]] + ([attn_pe_last] if tt in (0, 1) else []))
                v1 = p.v(lambda e, sl=sl: e.tensor_add(OUTST[sl][:, :], PS[:, sl * 1024:sl * 1024 + 1024], BOUTB[:, :]),
                         waits=[mm, st_last[sl]] + ([vtm_last, att_last] if tt in (0, 1) else []))
                sps_last[sl] = v1
                v2 = p.v(lambda e, sl=sl: e.tensor_mul(OUTST[sl][:, :], OUTST[sl][:, :], GSB[sl][:, :]),
                         waits=[v1, gs])
                comb_g_last[sl] = v2
                v3 = p.v(lambda e, sl=sl: e.tensor_add(OUTST[sl][:, :], OUTST[sl][:, :], XST2[sl][:, :]),
                         waits=[v2, tok_x])
                comb_x_last[sl] = v3
                comb_v_last = v3
                nst[sl] += 16
                p.dma("sync",
                      lambda e, s=s, tt=tt, sl=sl: e.dma_start(out=out_h[s, tt * TT:(tt + 1) * TT, :],
                                                               in_=OUTST[sl][:, :]),
                      waits=[v3], qsem=q_st[sl])
                st_last[sl] = (q_st[sl], nst[sl])
            yps_last[0] = sps_last[0]
            yps_last[1] = sps_last[1]
            for _b in range(8):
                hbk_last[_b] = comb_g_last[1] if comb_g_last[1] else comb_g_last[0]

    p.emit("sync", lambda e: e.nop(), waits=[st_last[0], st_last[1],
                                             (q_st[0], nst[0]), (q_st[1], nst[1])])

    from contextlib import ExitStack
    with ExitStack() as stack:
        for sm in p.sems:
            if sm.n > 0 or sm is s_cc:
                sm.h = stack.enter_context(nc.semaphore(sm.name))
        for e in ENGINES:
            for sm, v in p.waited[e].items():
                assert sm.h is not None and v <= sm.n, (e, sm.name, v, sm.n)
        block = stack.enter_context(nc.Block())
        p.play(block)
    return nc


_NC_CACHE = {}


def _get_nc():
    if "nc" not in _NC_CACHE:
        _NC_CACHE["nc"] = build_program()
    return _NC_CACHE["nc"]


def kernel(L0_attn_x, L0_attn_Wqa, L0_attn_Wqb, L0_attn_Wkva, L0_attn_Wkvb,
           L0_attn_Wout, L0_attn_bout, L0_ffn_Wg, L0_ffn_bg, L0_ffn_W1, L0_ffn_W2):
    nc = _get_nc()
    x = np.ascontiguousarray(np.asarray(L0_attn_x, dtype=np.float32))
    shared = {
        "L0_attn_Wqa": np.ascontiguousarray(L0_attn_Wqa, dtype=np.float32),
        "L0_attn_Wqb": np.ascontiguousarray(L0_attn_Wqb, dtype=np.float32),
        "L0_attn_Wkva": np.ascontiguousarray(L0_attn_Wkva, dtype=np.float32),
        "L0_attn_Wkvb": np.ascontiguousarray(L0_attn_Wkvb, dtype=np.float32),
        "L0_attn_Wout": np.ascontiguousarray(L0_attn_Wout, dtype=np.float32),
        "L0_attn_bout": np.ascontiguousarray(L0_attn_bout, dtype=np.float32),
        "L0_ffn_Wg": np.ascontiguousarray(L0_ffn_Wg, dtype=np.float32),
        "L0_ffn_bg": np.ascontiguousarray(L0_ffn_bg, dtype=np.float32),
        "L0_ffn_W1": np.ascontiguousarray(L0_ffn_W1, dtype=np.float32),
        "L0_ffn_W2": np.ascontiguousarray(L0_ffn_W2, dtype=np.float32),
    }
    in_maps = []
    for c in range(NCORES):
        m = dict(shared)
        m["L0_attn_x"] = np.ascontiguousarray(x[c * SPC:(c + 1) * SPC])
        in_maps.append(m)
    res = run_bass_kernel_spmd(nc, in_maps, core_ids=list(range(NCORES)))
    return np.concatenate([r["out"] for r in res.results], axis=0)
